# version 33
# baseline (speedup 1.0000x reference)
import numpy as np
import ml_dtypes
import concourse.bass as bass
import concourse.mybir as mybir
from concourse.bass_utils import run_bass_kernel_spmd

F32 = mybir.dt.float32
BF16 = mybir.dt.bfloat16
I32 = mybir.dt.int32
AF = mybir.ActivationFunctionType
ALU = mybir.AluOpType
AX = mybir.AxisListType

D = 1024
NCH = 8
T = 256
NT = 16
POS = 2048
FF = 4096
NF = 32
EPS = 1e-5
L2EPS = 1e-6
NE = 24

_STAGE = "full"
_NTRUN = NT

DQ_UNITS = []
for _xp in range(4):
    for _g in range(8):
        DQ_UNITS.append(("w1", _g, _xp))
    for _g in range(8 * _xp, 8 * _xp + 8):
        DQ_UNITS.append(("w2", _g, 0))
N_DQ = len(DQ_UNITS)
W1_READY = {}
W2_READY = {}
for _i, (_k, _g, _x) in enumerate(DQ_UNITS):
    if _k == "w1":
        for _fs in (2 * _x, 2 * _x + 1):
            W1_READY[_fs] = max(W1_READY.get(_fs, 0), _i + 1)
    else:
        W2_READY[_g] = max(W2_READY.get(_g, 0), _i + 1)


def _build():
    NTA = _NTRUN if _STAGE in ("DQ", "A") else NT
    ST_A = _STAGE != "DQ"
    ST_B = _STAGE in ("B", "D1", "full")
    ST_D1 = _STAGE in ("D1", "full")
    ST_D2 = _STAGE == "full"
    nc = bass.Bass("TRN2", target_bir_lowering=False, debug=False,
                   detect_race_conditions=False)

    xin = nc.dram_tensor("xin", [NT, 128, NCH * T], F32, kind="ExternalInput").ap()
    q1h = nc.dram_tensor("q1", [D, FF], I32, kind="ExternalInput").ap()
    s1h = nc.dram_tensor("s1", [8, FF], F32, kind="ExternalInput").ap()
    b1h = nc.dram_tensor("b1", [128, NF], F32, kind="ExternalInput").ap()
    q2h = nc.dram_tensor("q2", [FF, D], I32, kind="ExternalInput").ap()
    s2h = nc.dram_tensor("s2", [32, D], F32, kind="ExternalInput").ap()
    b2h = nc.dram_tensor("b2", [128, NCH], F32, kind="ExternalInput").ap()
    gfh = nc.dram_tensor("gf", [128, NCH], F32, kind="ExternalInput").ap()
    bfh = nc.dram_tensor("bf", [128, NCH], F32, kind="ExternalInput").ap()
    gah = nc.dram_tensor("ga", [128, NCH], F32, kind="ExternalInput").ap()
    bah = nc.dram_tensor("ba", [128, NCH], F32, kind="ExternalInput").ap()
    winh = nc.dram_tensor("win", [D, 3 * D], F32, kind="ExternalInput").ap()
    wouth = nc.dram_tensor("wout", [D, D], F32, kind="ExternalInput").ap()
    csth = nc.dram_tensor("cst", [8, 128, 4 * T], F32, kind="ExternalInput").ap()
    outh = nc.dram_tensor("outp", [NT, 128, NCH * T], F32, kind="ExternalOutput").ap()

    x1sp = nc.dram_tensor("x1sp", [NT, 128, NCH * T], F32, kind="Internal").ap()
    qnsp = nc.dram_tensor("qnsp", [NT, 128, NCH * T], F32, kind="Internal").ap()
    ysp = nc.dram_tensor("ysp", [NT, 128, NCH * T], F32, kind="Internal").ap()
    w1bk = nc.dram_tensor("w1bk", [8, 128, FF], BF16, kind="Internal").ap()
    stsp = nc.dram_tensor("stsp", [NT, 1024], F32, kind="Internal").ap()
    arint = nc.dram_tensor("arin_", [128, 16], F32, kind="Internal")
    aroutt = nc.dram_tensor("arout_", [128, 16], F32, kind="Internal")
    arin = arint.ap()
    arout = aroutt.ap()

    arena = nc.alloc_sbuf_tensor("arena", [128, 65536], BF16)
    arena2 = nc.alloc_sbuf_tensor("arena2", [128, 7168], F32)
    xbuf = nc.alloc_sbuf_tensor("xbuf", [128, 2 * NCH * T], F32)
    h1buf = nc.alloc_sbuf_tensor("h1buf", [128, 2 * NCH * T], BF16)
    y1buf = nc.alloc_sbuf_tensor("y1buf", [128, 8 * T], BF16)
    sqt = nc.alloc_sbuf_tensor("sqt", [128, 2 * T], F32)
    tscr = nc.alloc_sbuf_tensor("tscr", [128, 2 * T], F32)
    rowsf = nc.alloc_sbuf_tensor("rowsf", [1, 2048], F32)
    rowsB = nc.alloc_sbuf_tensor("rowsB", [1, 1024], F32)
    costg = nc.alloc_sbuf_tensor("costg", [128, 4 * T], F32)
    b1c = nc.alloc_sbuf_tensor("b1c", [128, NF], F32)
    b2c = nc.alloc_sbuf_tensor("b2c", [128, NCH], F32)
    qb2c = nc.alloc_sbuf_tensor("qb2c", [128, NCH], F32)
    gfc = nc.alloc_sbuf_tensor("gfc", [128, NCH], F32)
    bfc = nc.alloc_sbuf_tensor("bfc", [128, NCH], F32)
    gac = nc.alloc_sbuf_tensor("gac", [128, NCH], F32)
    bac = nc.alloc_sbuf_tensor("bac", [128, NCH], F32)
    onesc = nc.alloc_sbuf_tensor("onesc", [128, 1], F32)
    epsc = nc.alloc_sbuf_tensor("epsc", [128, 1], F32)
    l2c = nc.alloc_sbuf_tensor("l2c", [128, 1], F32)
    onesr = nc.alloc_sbuf_tensor("onesr", [1, 128], F32)
    a2t = nc.alloc_sbuf_tensor("a2t", [128, 16], F32)
    aft = nc.alloc_sbuf_tensor("aft", [128, 16], F32)
    a2acc = nc.alloc_sbuf_tensor("a2acc", [128, 128], F32)
    bcsb = nc.alloc_sbuf_tensor("bcsb", [128, 512], F32)

    zb = [nc.alloc_psum_tensor(f"zb{i}", [128, 512], F32) for i in range(4)]
    hb = [nc.alloc_psum_tensor(f"hb{i}", [128, 512], F32) for i in range(2)]
    stp = nc.alloc_psum_tensor("stp", [1, 512], F32)
    bcp = nc.alloc_psum_tensor("bcp", [128, 512], F32)

    ROT = ([(zb[k], 0) for k in range(4)] + [(hb[0], 0)] +
           [(zb[k], 256) for k in range(4)])

    def rot_slot(i):
        t_, off = ROT[i % 9]
        return t_[:, off:off + T]

    ROT10 = ROT + [(hb[1], 0)]

    def rot10(i):
        t_, off = ROT10[i % 10]
        return t_[:, off:off + T]

    names = ["dq_v", "w1bkd", "sqA", "s2mm", "zrA", "zrB",
             "rw1", "sq1", "rw2", "sq2", "rowA", "bch", "h1t", "h1a",
             "hid", "y1", "zev", "bcx", "x1a", "sqB", "s2mmB",
             "rwB1", "sqB1", "rwB2", "sqB2", "rowB",
             "xb", "stl", "cs", "bch2", "h2t", "h2a", "qkvm", "qkve",
             "sqK", "s2K", "ropQ", "sqQ", "s2Q",
             "sqKQ", "rkq", "bckq", "qnv", "qnsp_", "arr", "cc", "af",
             "winl", "woutl", "wa", "ym", "ye",
             "w1r", "std", "bcu", "x2a", "cst1",
             "cst0", "dq_q0", "dq_q1", "dq_q2", "dq_q3",
             "dq_s0", "dq_s1", "dq_s2",
             "xa0", "xa1", "xb0", "xb1", "yd0", "yd1",
             "qn10", "qn11", "x1sp0", "x1sp1",
             "owA0", "owA1", "owB0", "owB1", "xdA0", "xdA1", "xdB0", "xdB1",
             "ysp0", "ysp1", "stw0", "stw1", "bcc", "kna", "knvd", "sqD"]
    if _STAGE in ("A", "D1", "DQ"):
        names += ["outw0", "outw1"]
    if _STAGE == "B":
        names += ["afd"]
    sem = {n: nc.alloc_semaphore(name="s_" + n) for n in names}

    def sp(base, j):
        return sem[base + str(j % 2)]

    def pcnt(j):
        return 16 * (j // 2 + 1)

    def sp4(base, j):
        return sem[base + str(j % 4)]

    QSPF = 64 if _STAGE == "B" else 32

    def xsl(j):
        return xbuf[:, (j % 2) * NCH * T:(j % 2 + 1) * NCH * T]

    def xch(j, c):
        return xbuf[:, (j % 2) * NCH * T + c * T:(j % 2) * NCH * T + (c + 1) * T]

    def h1ch(j, c):
        return h1buf[:, (j % 2) * NCH * T + c * T:(j % 2) * NCH * T + (c + 1) * T]

    def w1t(g, f):
        return arena[:, g * FF + f * 128:g * FF + (f + 1) * 128]

    def w2t(g, zc):
        return arena[:, 32768 + g * D + zc * 128:32768 + g * D + (zc + 1) * 128]

    def wint(dc, ec):
        return arena[:, dc * 3072 + ec * 128:dc * 3072 + (ec + 1) * 128]

    def qnD1(j, dc):
        return arena[:, (j % 2) * 2048 + dc * T:(j % 2) * 2048 + (dc + 1) * T]

    def a2Q(c):
        return arena2[:, c * T:(c + 1) * T]

    def a2K(c):
        return arena2[:, 2048 + c * T:2048 + (c + 1) * T]

    def a2V(c):
        return arena2[:, 4096 + c * T:4096 + (c + 1) * T]

    def a2rot(i):
        return arena2[:, 6144 + i * T:6144 + (i + 1) * T]

    def a2Y(j, c):
        return arena2[:, (j % 2) * 2048 + c * T:(j % 2) * 2048 + (c + 1) * T]

    def a2Ysl(j):
        return arena2[:, (j % 2) * 2048:(j % 2) * 2048 + 2048]

    def sbcsl(i):
        return arena2[:, (i % 3) * 1024:(i % 3) * 1024 + 1024]

    def qfsl(i):
        return arena2[:, 3072 + (i % 4) * 1024:3072 + (i % 4) * 1024 + 1024]

    ECO = list(range(8, 16)) + list(range(0, 8)) + list(range(16, 24))

    R_M, R_V, R_S1, R_S2 = 0, 256, 512, 768
    R_AH, R_BH, R_AU, R_BU = 1024, 1280, 1536, 1792
    B_AH, B_BH, B_R2, B_MR, B_EV = 0, 256, 512, 768, 1024

    def sqsl(c):
        return sqt[:, (c % 2) * T:(c % 2 + 1) * T]

    def rf(off, n=256):
        return rowsf[0:1, off:off + n]

    def rb(off, n=256):
        return rowsB[0:1, off:off + n]

    def dq_dst(kind, g, x):
        if kind == "w1":
            return arena[:, g * FF + x * 1024:g * FF + (x + 1) * 1024]
        return arena[:, 32768 + g * D:32768 + (g + 1) * D]

    def emit_sync(e):
        e.dma_start(out=b1c[:], in_=b1h[:]).then_inc(sem["cst0"], 16)
        e.dma_start(out=b2c[:], in_=b2h[:]).then_inc(sem["cst0"], 16)
        e.dma_start(out=gfc[:], in_=gfh[:]).then_inc(sem["cst0"], 16)
        e.dma_start(out=bfc[:], in_=bfh[:]).then_inc(sem["cst0"], 16)
        e.dma_start(out=gac[:], in_=gah[:]).then_inc(sem["cst0"], 16)
        e.dma_start(out=bac[:], in_=bah[:]).then_inc(sem["cst0"], 16)
        if ST_A:
            e.dma_start(out=xsl(0), in_=xin[0]).then_inc(sem["xa0"], 16)
            if NTA > 1:
                e.dma_start(out=xsl(1), in_=xin[1]).then_inc(sem["xa1"], 16)
        for i, (kind, g, x) in enumerate(DQ_UNITS):
            if i >= 3:
                e.wait_ge(sem["dq_v"], i - 2)
            if kind == "w1":
                src = s1h[g:g + 1, x * 1024:(x + 1) * 1024]
            else:
                src = s2h[g:g + 1, 0:1024]
            e.dma_start(out=sbcsl(i), in_=src.partition_broadcast(128)
                        ).then_inc(sem["dq_s" + str(i % 3)], 16)
        for j in range(2, NTA if ST_A else 0):
            e.wait_ge(sp("x1sp", j), pcnt(j - 2))
            e.wait_ge(sem["s2mmB"], 8 * (j - 1))
            if _STAGE == "A":
                e.wait_ge(sp("outw", j), pcnt(j - 2))
            e.dma_start(out=xsl(j), in_=xin[j]).then_inc(sp("xa", j), 16)
        def qn_spill(jj):
            e.wait_ge(sem["qnv"], jj + 1)
            e.dma_start(out=qnsp[jj][:, 0:1024],
                        in_=arena2[:, 6144:7168]).then_inc(sem["qnsp_"], 16)
            e.dma_start(out=qnsp[jj][:, 1024:2048],
                        in_=arena2[:, 1024:2048]).then_inc(sem["qnsp_"], 16)
            if _STAGE == "B":
                e.dma_start(out=outh[jj][:, 0:1024],
                            in_=arena2[:, 6144:7168]).then_inc(sem["qnsp_"], 16)
                e.dma_start(out=outh[jj][:, 1024:2048],
                            in_=arena2[:, 1024:2048]).then_inc(sem["qnsp_"], 16)

        for j in range(NT if ST_B else 0):
            if j == 0:
                e.wait_ge(sem["x1sp0"], 16 * (NT // 2))
                e.wait_ge(sem["s2mmB"], 8 * (NT - 1))
            elif j == 1:
                e.wait_ge(sem["x1sp1"], 16 * (NT // 2))
                e.wait_ge(sem["s2mmB"], 8 * NT)
            else:
                e.wait_ge(sem["h2t"], 16 * (j - 1))
            e.dma_start(out=xsl(j), in_=x1sp[j]).then_inc(sp("xb", j), 16)
            if j == 0:
                e.wait_ge(sem["stw0"], 16 * (NT // 2))
                e.wait_ge(sem["stw1"], 16 * (NT // 2))
            else:
                e.wait_ge(sem["bch2"], j)
            e.dma_start(out=rowsB[0:1, 0:1024],
                        in_=stsp[j:j + 1, 0:1024]).then_inc(sem["stl"], 16)
            if j >= 2:
                qn_spill(j - 2)
            e.wait_ge(sem["ropQ"], j)
            e.dma_start(out=costg[:], in_=csth[j % 8]).then_inc(sem["cs"], 16)
        if ST_B:
            qn_spill(NT - 2)
            qn_spill(NT - 1)
        for j in range(NT if ST_D2 else 0):
            if j == 0:
                e.wait_ge(sem["bch2"], NT)
            else:
                e.wait_ge(sem["bcu"], j)
            e.dma_start(out=rowsB[0:1, 0:512],
                        in_=stsp[j:j + 1, 512:1024]).then_inc(sem["std"], 16)
            if j < 2:
                e.wait_ge(sem["qnv"], NT)
            else:
                e.wait_ge(sp("owA", j), pcnt(j - 2))
            e.dma_start(out=xsl(j)[:, 0:4 * T],
                        in_=x1sp[j][:, 0:4 * T]).then_inc(sp("xdA", j), 16)
            if j >= 2:
                e.wait_ge(sp("owB", j), pcnt(j - 2))
            e.dma_start(out=xsl(j)[:, 4 * T:8 * T],
                        in_=x1sp[j][:, 4 * T:8 * T]).then_inc(sp("xdB", j), 16)
            e.wait_ge(sem["ysp0"], 16 * (NT // 2))
            e.wait_ge(sem["ysp1"], 16 * (NT // 2))
            if j >= 2:
                e.wait_ge(sem["x2a"], 8 * (j - 1))
            e.dma_start(out=a2Ysl(j), in_=ysp[j]).then_inc(sp("yd", j), 16)
        if _STAGE == "B":
            e.wait_ge(sem["qnsp_"], QSPF * NT)
            e.wait_ge(sem["afd"], 32)
        elif _STAGE == "DQ":
            e.wait_ge(sem["outw0"], 16)
        elif _STAGE in ("A", "D1"):
            nw = NTA if _STAGE == "A" else NT
            e.wait_ge(sem["outw0"], 16 * ((nw + 1) // 2))
            if nw > 1:
                e.wait_ge(sem["outw1"], 16 * (nw // 2))
        else:
            for s in ("owA0", "owA1", "owB0", "owB1"):
                e.wait_ge(sem[s], 16 * (NT // 2))

    def emit_gpsimd(e):
        for i, (kind, g, x) in enumerate(DQ_UNITS):
            if i >= 4:
                e.wait_ge(sem["dq_v"], i - 3)
            if kind == "w1":
                src = q1h[g * 128:(g + 1) * 128, x * 1024:(x + 1) * 1024]
            else:
                src = q2h[g * 128:(g + 1) * 128, 0:1024]
            e.dma_start(out=qfsl(i), in_=src).then_inc(
                sem["dq_q" + str(i % 4)], 16)
        e.wait_ge(sem["dq_v"], N_DQ)
        for g in range(8):
            e.dma_start(out=w1bk[g], in_=arena[:, g * FF:(g + 1) * FF]
                        ).then_inc(sem["w1bkd"], 16)
        if _STAGE == "DQ":
            e.wait_ge(sem["w1bkd"], 16 * 8)
            e.dma_start(out=outh[0][:, 0:2048],
                        in_=arena[:, 0:2048]).then_inc(sem["outw0"], 16)
            return
        if not ST_B:
            return
        e.wait_ge(sem["zrB"], 32 * NT)
        for dc in range(NCH):
            e.dma_start(out=arena[:, dc * 3072:(dc + 1) * 3072],
                        in_=winh[dc * 128:(dc + 1) * 128, :]).then_inc(sem["winl"], 16)
        for dc in range(NCH):
            e.dma_start(out=arena[:, 24576 + dc * D:24576 + (dc + 1) * D],
                        in_=wouth[dc * 128:(dc + 1) * 128, :]).then_inc(sem["woutl"], 16)
        e.wait_ge(sem["qnv"], NT + 1)
        e.dma_start(out=arin[:], in_=a2t[:]).then_inc(sem["arr"], 16)
        e.wait_ge(sem["arr"], 16)
        e.collective_compute(
            "AllReduce", ALU.add,
            ins=[arint.ap().opt()], outs=[aroutt.ap().opt()],
            replica_groups=[[0, 1, 2, 3, 4, 5, 6, 7]],
        ).then_inc(sem["cc"])
        e.wait_ge(sem["cc"], 1)
        e.dma_start(out=aft[:], in_=arout[:]).then_inc(sem["af"], 16)
        if _STAGE == "B":
            e.wait_ge(sem["qnsp_"], QSPF * NT)
            e.dma_start(out=outh[0][:, 2032:2048], in_=aft[:]).then_inc(sem["afd"], 16)
            e.dma_start(out=outh[1][:, 2032:2048], in_=arin[:]).then_inc(sem["afd"], 16)
        if not ST_D1:
            return
        for j in range(NT):
            if j == 0:
                e.wait_ge(sem["qnsp_"], QSPF * NT)
            if j >= 2:
                e.wait_ge(sem["ym"], 8 * (j - 1))
            e.dma_start(out=arena[:, (j % 2) * 2048:(j % 2) * 2048 + 2048],
                        in_=qnsp[j]).then_inc(sp("qn1", j), 16)
        e.wait_ge(sem["ym"], 8 * NT)
        for g in range(8):
            e.dma_start(out=arena[:, g * FF:(g + 1) * FF],
                        in_=w1bk[g]).then_inc(sem["w1r"], 16)
        if ST_D2:
            for jt in range(NT):
                for c in range(NCH):
                    e.wait_ge(sem["x2a"], 8 * jt + c + 1)
                    if c >= 2:
                        e.wait_ge(sem["s2mm"], 8 * (NT + jt) + c - 1)
                    elif jt >= 1:
                        e.wait_ge(sem["s2mm"], 8 * (NT + jt))
                    e.tensor_mul(sqsl(c), xch(jt, c), xch(jt, c)).then_inc(
                        sem["sqD"], 1)

    def _z_half(e, gi, half):
        fi = gi % NF
        if fi == 0 and gi >= NF:
            e.wait_ge(sem["zev"], 8 * (gi // NF - 1) + 4 * (half + 1))
        e.wait_ge(sem["y1"], gi + 1)
        buf = y1buf[:, (gi % 8) * T:(gi % 8 + 1) * T]
        ins = None
        for z in range(4 * half, 4 * half + 4):
            bank, off = z // 2, (z % 2) * T
            ins = e.matmul(zb[bank][:, off:off + T], w2t(fi, z), buf,
                           start=(fi == 0 and z % 2 == 0),
                           stop=(fi == NF - 1),
                           skip_group_check=True)
        ins.then_inc(sem["zrA" if half == 0 else "zrB"], 1)

    def _pe_s1(e, jslot):
        for c in range(NCH):
            e.matmul(stp[0:1, 0:T], onesc[:, 0:1], xch(jslot, c),
                     start=(c == 0), stop=(c == NCH - 1))

    def _pe_s2(e, c, count_sem):
        ins = e.matmul(stp[0:1, T:2 * T], onesc[:, 0:1], sqsl(c),
                       start=False, stop=(c == NCH - 1),
                       skip_group_check=True)
        ins.then_inc(sem[count_sem], 1)

    def emit_ffn_phase_pe(e, ph, ntiles):
        base = ph * NT
        if ph == 0:
            e.wait_ge(sem["xa0"], 16)
            _pe_s1(e, 0)
            for c in range(NCH):
                e.wait_ge(sem["sqA"], c + 1)
                _pe_s2(e, c, "s2mm")
            e.wait_ge(sem["rowA"], 1)
            e.matmul(bcp[:, 0:512], onesr[0:1, :], rf(R_AH, 512),
                     start=True, stop=True).then_inc(sem["bch"], 1)
        else:
            e.wait_ge(sem["std"], 16)
            e.matmul(bcp[:, 0:512], onesr[0:1, :], rb(0, 512),
                     start=True, stop=True).then_inc(sem["bcu"], 1)
            e.wait_ge(sem["x2a"], 8)
            _pe_s1(e, 0)
            for c in range(NCH):
                e.wait_ge(sem["sqD"], c + 1)
                _pe_s2(e, c, "s2mm")
            e.wait_ge(sem["rowA"], NT + 1)
            e.matmul(bcp[:, 0:512], onesr[0:1, :], rf(R_AH, 512),
                     start=True, stop=True).then_inc(sem["bch"], 1)
        for j in range(ntiles):
            ja = base + j
            last = (j == ntiles - 1)
            for f in range(NF):
                gi = ja * NF + f
                if f == 0:
                    e.wait_ge(sem["h1a"], 8 * (ja + 1))
                    if ph == 1 and j == 0:
                        e.wait_ge(sem["w1r"], 16 * 8)
                if ph == 0 and j == 0 and f % 4 == 0:
                    e.wait_ge(sem["dq_v"], W1_READY[f // 4])
                if gi >= 2:
                    e.wait_ge(sem["y1"], gi - 1)
                ins = None
                for k in range(NCH):
                    ins = e.matmul(hb[f % 2][:, 0:T], w1t(k, f), h1ch(j, k),
                                   start=(k == 0), stop=(k == NCH - 1))
                ins.then_inc(sem["hid"], 1)
                if f >= 4:
                    if ph == 0 and j == 0:
                        e.wait_ge(sem["dq_v"], W2_READY[(gi - 4) % NF])
                    _z_half(e, gi - 4, 0)
                if f >= 7:
                    _z_half(e, gi - 7, 1)
                if ph == 0:
                    if f == 2:
                        e.wait_ge(sem["h1t"], 16 * (ja + 1))
                        e.wait_ge(sem["rowA"], ja + 1)
                        e.matmul(bcp[:, 0:512], onesr[0:1, :], rf(R_AU, 512),
                                 start=True, stop=True).then_inc(sem["bcx"], 1)
                    if 1 <= f <= 8 and j >= 1:
                        c = f - 1
                        e.wait_ge(sem["x1a"], 8 * (ja - 1) + c + 1)
                        if c == 0:
                            e.wait_ge(sem["rw1"], ja + 1)
                        e.matmul(stp[0:1, 0:T], onesc[:, 0:1], xch(j - 1, c),
                                 start=(c == 0), stop=(c == NCH - 1))
                    if 3 <= f <= 10 and j >= 1:
                        c = f - 3
                        e.wait_ge(sem["sqB"], 8 * (j - 1) + c + 1)
                        _pe_s2(e, c, "s2mmB")
                    if f == 13 and not last:
                        e.wait_ge(sp("xa", j + 1), pcnt(j + 1))
                        e.wait_ge(sem["rwB1"], j)
                        e.wait_ge(sem["rw1"], ja + 1)
                        _pe_s1(e, j + 1)
                    if 15 <= f <= 22 and not last:
                        c = f - 15
                        e.wait_ge(sem["sqA"], 8 * (ja + 1) + c + 1)
                        _pe_s2(e, c, "s2mm")
                    if f == 27 and not last:
                        e.wait_ge(sem["rowA"], ja + 2)
                        e.wait_ge(sem["bcc"], ja + 1)
                        e.matmul(bcp[:, 0:512], onesr[0:1, :], rf(R_AH, 512),
                                 start=True, stop=True).then_inc(sem["bch"], 1)
                else:
                    if f == 2 and not last:
                        e.wait_ge(sem["std"], 16 * (j + 2))
                        e.wait_ge(sem["h1t"], 16 * (ja + 1))
                        e.matmul(bcp[:, 0:512], onesr[0:1, :], rb(0, 512),
                                 start=True, stop=True).then_inc(sem["bcu"], 1)
                    if 16 <= f <= 23 and not last:
                        c = f - 16
                        e.wait_ge(sem["x2a"], 8 * (j + 1) + c + 1)
                        if c == 0:
                            e.wait_ge(sem["rw1"], ja + 1)
                        e.matmul(stp[0:1, 0:T], onesc[:, 0:1], xch(j + 1, c),
                                 start=(c == 0), stop=(c == NCH - 1))
                    if f == 20:
                        if not last:
                            e.wait_ge(sem["x2a"], 8 * (j + 2))
                        e.wait_ge(sem["h1t"], 16 * (ja + 1))
                        e.wait_ge(sem["rowA"], ja + 1)
                        e.matmul(bcp[:, 0:512], onesr[0:1, :], rf(R_AU, 512),
                                 start=True, stop=True).then_inc(sem["bcx"], 1)
                    if 18 <= f <= 25 and not last:
                        c = f - 18
                        e.wait_ge(sem["sqD"], 8 * (j + 1) + c + 1)
                        _pe_s2(e, c, "s2mm")
                    if f == 29 and not last:
                        e.wait_ge(sem["rowA"], ja + 2)
                        e.wait_ge(sem["bcc"], ja + 1)
                        e.matmul(bcp[:, 0:512], onesr[0:1, :], rf(R_AH, 512),
                                 start=True, stop=True).then_inc(sem["bch"], 1)
            for d in range(4):
                if ph == 0 and j == 0:
                    e.wait_ge(sem["dq_v"], W2_READY[28 + d])
                _z_half(e, ja * NF + 28 + d, 0)
                _z_half(e, ja * NF + 25 + d, 1)
            for d in range(3):
                _z_half(e, ja * NF + 29 + d, 1)
        if ph == 0:
            j = ntiles - 1
            e.wait_ge(sem["x1a"], 8 * (base + ntiles))
            e.wait_ge(sem["rw1"], base + ntiles)
            _pe_s1(e, j)
            for c in range(NCH):
                e.wait_ge(sem["sqB"], 8 * j + c + 1)
                _pe_s2(e, c, "s2mmB")

    def emit_tensor(e):
        e.wait_ge(sem["cst1"], 1)
        if not ST_A:
            return
        emit_ffn_phase_pe(e, 0, NTA)
        if not ST_B:
            return
        e.wait_ge(sem["stl"], 16)
        e.wait_ge(sem["bcc"], NT)
        e.matmul(bcp[:, 0:512], onesr[0:1, :], rb(B_AH, 512),
                 start=True, stop=True).then_inc(sem["bch2"], 1)
        for j in range(NT):
            for p, ec in enumerate(ECO):
                i = j * NE + p
                if i % 9 <= 3:
                    if i >= 4:
                        e.wait_ge(sem["qkve"], i - 3)
                elif i >= 5:
                    e.wait_ge(sem["qkve"], i - 4)
                if p == 0:
                    e.wait_ge(sem["h2a"], 8 * (j + 1))
                    if j == 0:
                        e.wait_ge(sem["winl"], 16 * NCH)
                        e.wait_ge(sem["zev"], 8 * NT)
                        e.wait_ge(sem["y1"], 32 * NT)
                ins = None
                for dc in range(NCH):
                    ins = e.matmul(rot_slot(i), wint(dc, ec), h1ch(j, dc),
                                   start=(dc == 0), stop=(dc == NCH - 1))
                ins.then_inc(sem["qkvm"], 1)
                if p == 1 and j < NT - 1:
                    e.wait_ge(sem["stl"], 16 * (j + 2))
                    e.wait_ge(sem["h2t"], 16 * (j + 1))
                    e.matmul(bcp[:, 0:512], onesr[0:1, :], rb(B_AH, 512),
                             start=True, stop=True).then_inc(sem["bch2"], 1)
                if 4 <= p <= 11:
                    c = p - 4
                    if c == 0:
                        e.wait_ge(sem["sqKQ"], j)
                    e.wait_ge(sem["sqK"], 8 * j + c + 1)
                    ins2 = e.matmul(stp[0:1, 0:T], onesc[:, 0:1], sqsl(c),
                                    start=(c == 0), stop=(c == NCH - 1))
                    ins2.then_inc(sem["s2K"], 1)
                if 12 <= p <= 19:
                    c = p - 12
                    e.wait_ge(sem["sqQ"], 8 * j + c + 1)
                    ins2 = e.matmul(stp[0:1, T:2 * T], onesc[:, 0:1], sqsl(c),
                                    start=(c == 0), stop=(c == NCH - 1))
                    ins2.then_inc(sem["s2Q"], 1)
                if p == 22:
                    e.wait_ge(sem["rkq"], j + 1)
                    if j >= 1:
                        e.wait_ge(sem["knvd"], j)
                    e.matmul(hb[1][:, 0:512], onesr[0:1, :], rf(R_S1, 512),
                             start=True, stop=True).then_inc(sem["bckq"], 1)
        if not ST_D1:
            return
        for j in range(NT):
            e.wait_ge(sp("qn1", j), pcnt(j))
            if j == 0:
                e.wait_ge(sem["wa"], 2)
            base = 24576 if j < 8 else 8192
            for ec in range(NCH):
                i = j * NCH + ec
                if i >= 5:
                    e.wait_ge(sem["ye"], i - 4)
                ins = None
                for dc in range(NCH):
                    ins = e.matmul(
                        rot10(i),
                        arena[:, base + dc * D + ec * 128:base + dc * D + (ec + 1) * 128],
                        qnD1(j, dc), start=(dc == 0), stop=(dc == NCH - 1))
                ins.then_inc(sem["ym"], 1)
        if not ST_D2:
            return
        emit_ffn_phase_pe(e, 1, NT)

    def _achain(e, ja):
        e.wait_ge(sem["s2mm"], 8 * (ja + 1))
        e.tensor_scalar_mul(rf(R_M), stp[0:1, 0:T], 1.0 / D)
        e.tensor_mul(rf(R_S1), rf(R_M), rf(R_M))
        e.tensor_scalar_mul(rf(R_V), stp[0:1, T:2 * T], 1.0 / D)
        e.tensor_sub(rf(R_V), rf(R_V), rf(R_S1)).then_inc(sem["rw1"], 1)
        e.wait_ge(sem["sq1"], ja + 1)
        e.reciprocal(rf(R_S2), rf(R_S1))
        e.tensor_mul(rf(R_S1), rf(R_S2), rf(R_S2))
        e.tensor_mul(rf(R_S1), rf(R_V), rf(R_S1)).then_inc(sem["rw2"], 1)
        e.wait_ge(sem["sq2"], ja + 1)
        e.reciprocal(rf(R_S1), rf(R_V))
        e.tensor_mul(rf(R_AH), rf(R_S2), rf(R_S1))
        e.scalar_tensor_tensor(rf(R_BH), rf(R_M), -1.0, rf(R_AH),
                               ALU.mult, ALU.mult)
        e.tensor_scalar(rf(R_AU), rf(R_S2), 0.5, 1.0, ALU.mult, ALU.add)
        e.scalar_tensor_tensor(rf(R_BU), rf(R_M), -0.5, rf(R_S2),
                               ALU.mult, ALU.mult).then_inc(sem["rowA"], 1)

    def _bchain(e, j):
        e.wait_ge(sem["s2mmB"], 8 * (j + 1))
        if j >= 1:
            e.wait_ge(sp("stw", j - 1), pcnt(j - 1))
        e.tensor_scalar_mul(rf(R_M), stp[0:1, 0:T], 1.0 / D)
        e.tensor_mul(rf(R_S1), rf(R_M), rf(R_M))
        e.tensor_scalar_mul(rf(R_V), stp[0:1, T:2 * T], 1.0 / D)
        e.tensor_sub(rf(R_V), rf(R_V), rf(R_S1)).then_inc(sem["rwB1"], 1)
        e.wait_ge(sem["sqB1"], j + 1)
        e.reciprocal(rb(B_R2), rf(R_S1))
        e.tensor_mul(rf(R_S1), rb(B_R2), rb(B_R2))
        e.tensor_mul(rf(R_S1), rf(R_V), rf(R_S1)).then_inc(sem["rwB2"], 1)
        e.wait_ge(sem["sqB2"], j + 1)
        e.reciprocal(rf(R_V), rf(R_S2))
        e.tensor_mul(rb(B_AH), rb(B_R2), rf(R_V))
        e.scalar_tensor_tensor(rb(B_BH), rf(R_M), -1.0, rb(B_AH),
                               ALU.mult, ALU.mult)
        e.tensor_mul(rb(B_MR), rf(R_M), rb(B_R2))
        e.tensor_scalar(rb(B_R2), rb(B_R2), 1.0, 1.0, ALU.mult,
                        ALU.add).then_inc(sem["rowB"], 1)

    def _h_assembly(e, ja, hsem_t, hsem_a):
        for c in range(NCH):
            if c >= 2:
                e.wait_ge(sem[hsem_a], 8 * ja + c - 1)
            sl = tscr[:, (c % 2) * T:(c % 2 + 1) * T]
            e.tensor_mul(sl, xch(ja % NT, c), bcp[:, 0:T])
            e.tensor_add(sl, sl, bcp[:, T:2 * T]).then_inc(sem[hsem_t], 2)

    def _x_assembly(e, ja, last=False):
        e.wait_ge(sem["bcc"], ja + 1)
        e.wait_ge(sem["h1a"], 8 * (ja + 1 if last else ja + 2))
        e.wait_ge(sem["zrB"], 32 * (ja + 1))
        for c in range(NCH):
            if c >= 2:
                e.wait_ge(sem["x1a"], 8 * ja + c - 1)
            sl = tscr[:, (c % 2) * T:(c % 2 + 1) * T]
            bank, off = c // 2, (c % 2) * T
            e.tensor_mul(sl, xch(ja % NT, c), bcsb[:, 0:T])
            e.tensor_add(sl, sl, bcsb[:, T:2 * T])
            e.scalar_tensor_tensor(sl, zb[bank][:, off:off + T], 0.25, sl,
                                   ALU.mult, ALU.add).then_inc(sem["zev"], 1)

    def _bc_copy(e, ja):
        e.wait_ge(sem["bcx"], ja + 1)
        e.tensor_copy(bcsb[:, 0:T], bcp[:, 0:T])
        e.tensor_copy(bcsb[:, T:2 * T], bcp[:, T:2 * T]).then_inc(sem["bcc"], 1)

    def _x2_assembly(e, jt):
        e.wait_ge(sem["bcu"], jt + 1)
        e.wait_ge(sp("yd", jt), pcnt(jt))
        e.wait_ge(sem["h1a"], 8 * (NT + jt))
        if jt >= 1:
            e.wait_ge(sem["x1a"], 8 * (NT + jt - 1))
        for c in range(NCH):
            if c == 0:
                e.wait_ge(sp("xdA", jt), pcnt(jt))
            if c == 4:
                e.wait_ge(sp("xdB", jt), pcnt(jt))
            u = tscr[:, (c % 2) * T:(c % 2 + 1) * T]
            e.tensor_mul(u, xch(jt, c), bcp[:, 0:T])
            e.tensor_sub(u, u, bcp[:, T:2 * T])
            e.tensor_add(xch(jt, c), u, a2Y(jt, c)).then_inc(sem["x2a"], 1)

    def emit_ffn_phase_v(e, ph, ntiles, skip_prologue=False):
        base = ph * NT
        if ph == 0:
            if not skip_prologue:
                _achain(e, 0)
                e.wait_ge(sem["bch"], 1)
                _h_assembly(e, 0, "h1t", "h1a")
        else:
            _x2_assembly(e, 0)
            _achain(e, NT)
            e.wait_ge(sem["bch"], NT + 1)
            _h_assembly(e, NT, "h1t", "h1a")
        for j in range(ntiles):
            ja = base + j
            last = (j == ntiles - 1)
            if j >= 1:
                _x_assembly(e, ja - 1)
            if ph == 0:
                _bc_copy(e, ja)
                if j >= 1:
                    _bchain(e, j - 1)
                if not last:
                    _achain(e, ja + 1)
                    e.wait_ge(sem["bch"], ja + 2)
                    _h_assembly(e, ja + 1, "h1t", "h1a")
            else:
                if not last:
                    _x2_assembly(e, j + 1)
                    _bc_copy(e, ja)
                    _achain(e, ja + 1)
                    e.wait_ge(sem["bch"], ja + 2)
                    _h_assembly(e, ja + 1, "h1t", "h1a")
                else:
                    _bc_copy(e, ja)
        _x_assembly(e, base + ntiles - 1, last=True)
        if ph == 0:
            _bchain(e, ntiles - 1)

    def emit_vector(e):
        e.memset(onesc[:], 1.0)
        e.memset(onesr[:], 1.0)
        e.memset(epsc[:], EPS)
        e.memset(l2c[:], L2EPS)
        e.memset(a2t[:], 0.0).then_inc(sem["cst1"], 1)
        e.wait_ge(sem["cst0"], 96)
        e.tensor_scalar_mul(qb2c[:], b2c[:], 0.25)

        def dq_unit(i):
            kind, g, x = DQ_UNITS[i]
            e.wait_ge(sem["dq_q" + str(i % 4)], 16 * (i // 4 + 1))
            e.wait_ge(sem["dq_s" + str(i % 3)], 16 * (i // 3 + 1))
            e.scalar_tensor_tensor(dq_dst(kind, g, x), qfsl(i), -128.0, sbcsl(i),
                                   ALU.add, ALU.mult).then_inc(sem["dq_v"], 1)

        for i in range(8):
            dq_unit(i)
        if ST_A:
            _achain(e, 0)
            e.wait_ge(sem["bch"], 1)
            _h_assembly(e, 0, "h1t", "h1a")
        for i in range(8, N_DQ):
            dq_unit(i)
        if not ST_A:
            return
        emit_ffn_phase_v(e, 0, NTA, skip_prologue=True)
        if not ST_B:
            return
        def qn_mul(jt):
            e.wait_ge(sem["bckq"], jt + 1)
            for i in range(4):
                e.tensor_mul(a2rot(i), a2rot(i), hb[1][:, T:2 * T])
            ins = None
            for c in range(4, NCH):
                ins = e.tensor_mul(a2Q(c), a2Q(c), hb[1][:, T:2 * T])
            ins.then_inc(sem["qnv"], 1)

        def kv_rk(jt):
            ins = None
            for c in range(NCH):
                e.tensor_mul(a2V(c), a2V(c), hb[1][:, 0:T])
                ins = e.reduce_sum(a2acc[:, jt * 8 + c:jt * 8 + c + 1],
                                   a2V(c), axis=AX.X)
            ins.then_inc(sem["knvd"], 1)

        e.wait_ge(sp("xb", 0), pcnt(0))
        e.wait_ge(sem["bch2"], 1)
        _h_assembly(e, 0, "h2t", "h2a")
        for j in range(NT):
            if j < NT - 1:
                e.wait_ge(sp("xb", j + 1), pcnt(j + 1))
                e.wait_ge(sem["bch2"], j + 2)
                e.wait_ge(sem["h2a"], 8 * (j + 1))
                _h_assembly(e, j + 1, "h2t", "h2a")
            if j >= 1:
                qn_mul(j - 1)
                kv_rk(j - 1)
            e.wait_ge(sem["cs"], 16 * (j + 1))
            e.wait_ge(sem["qnsp_"], QSPF * j)
            e.wait_ge(sem["qkve"], NE * j + 8)
            e.wait_ge(sem["sqK"], 8 * j + 4)
            for pc in range(2):
                ke, ko = a2K(2 * pc), a2K(2 * pc + 1)
                cosr = costg[:, pc * T:(pc + 1) * T]
                sinr = costg[:, 512 + pc * T:512 + (pc + 1) * T]
                e.tensor_mul(a2rot(0), ke, cosr)
                e.tensor_mul(a2rot(1), ke, sinr)
                e.tensor_mul(a2rot(2), ko, cosr)
                e.tensor_mul(a2rot(3), ko, sinr)
                e.tensor_sub(ke, a2rot(0), a2rot(3))
                e.tensor_add(ko, a2rot(2), a2rot(1))
            e.wait_ge(sem["qkve"], NE * j + 16)
            if j < NT - 1:
                e.wait_ge(sem["h2a"], 8 * (j + 2))
            ins = None
            for pc in range(2):
                qe, qo = a2Q(2 * pc), a2Q(2 * pc + 1)
                cosr = costg[:, pc * T:(pc + 1) * T]
                sinr = costg[:, 512 + pc * T:512 + (pc + 1) * T]
                e.tensor_mul(a2rot(2 * pc), qe, cosr)
                e.tensor_mul(tscr[:, 0:T], qo, sinr)
                e.tensor_sub(a2rot(2 * pc), a2rot(2 * pc), tscr[:, 0:T])
                e.tensor_mul(a2rot(2 * pc + 1), qo, cosr)
                e.tensor_mul(tscr[:, 0:T], qe, sinr)
                ins = e.tensor_add(a2rot(2 * pc + 1), a2rot(2 * pc + 1),
                                   tscr[:, 0:T])
            ins.then_inc(sem["ropQ"], 1)
            e.wait_ge(sem["sqKQ"], j + 1)
            e.reciprocal(rf(R_S1), rf(R_M))
            e.reciprocal(rf(R_S2), rf(R_V)).then_inc(sem["rkq"], 1)
            e.wait_ge(sem["qkve"], NE * (j + 1))
            ins = None
            for c in range(NCH):
                ins = e.tensor_mul(a2V(c), a2V(c), a2K(c))
            ins.then_inc(sem["kna"], 1)
        qn_mul(NT - 1)
        kv_rk(NT - 1)
        ins = None
        for b in range(2):
            for c in range(NCH):
                strided = a2acc.rearrange("p (t e) -> p t e", e=8)[:, b * 8:(b + 1) * 8, c]
                ins = e.reduce_sum(a2t[:, b * 8 + c:b * 8 + c + 1], strided, axis=AX.X)
        ins.then_inc(sem["qnv"], 1)
        if not ST_D1:
            return
        e.wait_ge(sem["af"], 16)
        e.wait_ge(sem["woutl"], 16 * NCH)
        ins = None
        for dc in range(NCH):
            ins = e.tensor_scalar_mul(
                arena[:, 8192 + dc * D:8192 + (dc + 1) * D],
                arena[:, 24576 + dc * D:24576 + (dc + 1) * D],
                aft[:, 8 + dc:8 + dc + 1])
        ins.then_inc(sem["wa"], 1)
        ins = None
        for dc in range(NCH):
            ins = e.tensor_scalar_mul(
                arena[:, 24576 + dc * D:24576 + (dc + 1) * D],
                arena[:, 24576 + dc * D:24576 + (dc + 1) * D],
                aft[:, dc:dc + 1])
        ins.then_inc(sem["wa"], 1)
        if not ST_D2:
            return
        emit_ffn_phase_v(e, 1, NT)

    def emit_ffn_phase_act(e, ph, ntiles):
        base = ph * NT

        def affine(jt, c):
            e.wait_ge(sem["h1t"], 16 * (base + jt) + 2 * (c + 1))
            e.activation(h1ch(jt, c), tscr[:, (c % 2) * T:(c % 2 + 1) * T],
                         AF.Identity, bias=bfc[:, c:c + 1],
                         scale=gfc[:, c:c + 1]).then_inc(sem["h1a"], 1)

        def xmove(jt, c):
            e.wait_ge(sem["zev"], 8 * jt + c + 1)
            e.activation(xch(jt % NT, c), tscr[:, (c % 2) * T:(c % 2 + 1) * T],
                         AF.Identity, bias=qb2c[:, c:c + 1]).then_inc(
                sem["x1a"], 1)

        if ph == 0:
            e.wait_ge(sem["xa0"], 16)
            for c in range(NCH):
                if c >= 2:
                    e.wait_ge(sem["s2mm"], c - 1)
                e.activation(sqsl(c), xch(0, c), AF.Square).then_inc(
                    sem["sqA"], 1)
            e.wait_ge(sem["rw1"], 1)
            e.activation(rf(R_S1), rf(R_V), AF.Sqrt, bias=epsc[0:1, 0:1]).then_inc(
                sem["sq1"], 1)
            e.wait_ge(sem["rw2"], 1)
            e.activation(rf(R_V), rf(R_S1), AF.Sqrt, bias=epsc[0:1, 0:1]).then_inc(
                sem["sq2"], 1)
            for c in range(NCH):
                affine(0, c)
        else:
            e.wait_ge(sem["rw1"], NT + 1)
            e.activation(rf(R_S1), rf(R_V), AF.Sqrt, bias=epsc[0:1, 0:1]).then_inc(
                sem["sq1"], 1)
            e.wait_ge(sem["rw2"], NT + 1)
            e.activation(rf(R_V), rf(R_S1), AF.Sqrt, bias=epsc[0:1, 0:1]).then_inc(
                sem["sq2"], 1)
            for c in range(NCH):
                affine(0, c)
        for j in range(ntiles):
            ja = base + j
            last = (j == ntiles - 1)
            for f in range(NF):
                gi = ja * NF + f
                e.wait_ge(sem["hid"], gi + 1)
                if gi >= 8:
                    e.wait_ge(sem["zrB"], gi - 7)
                e.activation(y1buf[:, (gi % 8) * T:(gi % 8 + 1) * T],
                             hb[f % 2][:, 0:T], AF.Silu,
                             bias=b1c[:, f:f + 1]).then_inc(sem["y1"], 1)
                if f <= 7 and j >= 1:
                    xmove(ja - 1, f)
                if ph == 0:
                    if 2 <= f <= 9 and j >= 1:
                        c = f - 2
                        if c == 0:
                            e.wait_ge(sem["x1a"], 8 * (ja - 1) + 1)
                            e.wait_ge(sem["s2mm"], 8 * (ja + 1))
                        elif c == 1:
                            e.wait_ge(sem["x1a"], 8 * (ja - 1) + c + 1)
                            e.wait_ge(sem["s2mm"], 8 * (ja + 1))
                        else:
                            e.wait_ge(sem["x1a"], 8 * (ja - 1) + c + 1)
                            e.wait_ge(sem["s2mmB"], 8 * (j - 1) + c - 1)
                        e.activation(sqsl(c), xch(j - 1, c), AF.Square
                                     ).then_inc(sem["sqB"], 1)
                    if f == 10 and j >= 1:
                        e.wait_ge(sem["rwB1"], j)
                        e.activation(rf(R_S1), rf(R_V), AF.Sqrt,
                                     bias=epsc[0:1, 0:1]).then_inc(sem["sqB1"], 1)
                    if f == 11 and j >= 1:
                        e.wait_ge(sem["rwB2"], j)
                        e.activation(rf(R_S2), rf(R_S1), AF.Sqrt,
                                     bias=epsc[0:1, 0:1]).then_inc(sem["sqB2"], 1)
                    if f == 12 and j >= 1:
                        e.wait_ge(sem["rowB"], j)
                        e.dma_start(out=stsp[j - 1:j, 0:1024],
                                    in_=rowsB[0:1, 0:1024]).then_inc(
                            sp("stw", j - 1), 16)
                    if f == 8 and j >= 1:
                        e.wait_ge(sem["x1a"], 8 * ja)
                        e.dma_start(out=x1sp[j - 1], in_=xsl(j - 1)).then_inc(
                            sp("x1sp", j - 1), 16)
                        if _STAGE == "A":
                            e.dma_start(out=outh[j - 1], in_=xsl(j - 1)).then_inc(
                                sp("outw", j - 1), 16)
                    if 14 <= f <= 21 and not last:
                        c = f - 14
                        if c <= 1:
                            if c == 0:
                                e.wait_ge(sp("xa", j + 1), pcnt(j + 1))
                            if j >= 1:
                                e.wait_ge(sem["s2mmB"], 8 * j)
                            else:
                                e.wait_ge(sem["s2mm"], 8 * (ja + 1))
                        else:
                            e.wait_ge(sem["s2mm"], 8 * (ja + 1) + c - 1)
                        e.activation(sqsl(c), xch(j + 1, c), AF.Square
                                     ).then_inc(sem["sqA"], 1)
                    if f == 23 and not last:
                        e.wait_ge(sem["rw1"], ja + 2)
                        e.activation(rf(R_S1), rf(R_V), AF.Sqrt,
                                     bias=epsc[0:1, 0:1]).then_inc(sem["sq1"], 1)
                    if f == 24 and not last:
                        e.wait_ge(sem["rw2"], ja + 2)
                        e.activation(rf(R_V), rf(R_S1), AF.Sqrt,
                                     bias=epsc[0:1, 0:1]).then_inc(sem["sq2"], 1)
                    if 28 <= f <= 31 and not last:
                        affine(j + 1, f - 28)
                else:
                    if f == 4 and j >= 1:
                        e.wait_ge(sem["x1a"], 8 * (ja - 1) + 4)
                        e.dma_start(out=outh[j - 1][:, 0:4 * T],
                                    in_=xsl(j - 1)[:, 0:4 * T]).then_inc(
                            sp("owA", j - 1), 16)
                    if f == 8 and j >= 1:
                        e.wait_ge(sem["x1a"], 8 * ja)
                        e.dma_start(out=outh[j - 1][:, 4 * T:8 * T],
                                    in_=xsl(j - 1)[:, 4 * T:8 * T]).then_inc(
                            sp("owB", j - 1), 16)
                    if 17 <= f <= 24 and not last:
                        c = f - 17
                        e.wait_ge(sem["x2a"], 8 * (j + 1) + c + 1)
                        if c <= 1:
                            e.wait_ge(sem["s2mm"], 8 * (ja + 1))
                        else:
                            e.wait_ge(sem["s2mm"], 8 * (ja + 1) + c - 1)
                        e.activation(sqsl(c), xch(j + 1, c), AF.Square
                                     ).then_inc(sem["sqA"], 1)
                    if f == 26 and not last:
                        e.wait_ge(sem["rw1"], ja + 2)
                        e.activation(rf(R_S1), rf(R_V), AF.Sqrt,
                                     bias=epsc[0:1, 0:1]).then_inc(sem["sq1"], 1)
                    if f == 27 and not last:
                        e.wait_ge(sem["rw2"], ja + 2)
                        e.activation(rf(R_V), rf(R_S1), AF.Sqrt,
                                     bias=epsc[0:1, 0:1]).then_inc(sem["sq2"], 1)
                    if 30 <= f <= 31 and not last:
                        affine(j + 1, f - 30)
            if not last:
                for c in range((4 if ph == 0 else 2), NCH):
                    affine(j + 1, c)
        j = ntiles - 1
        ja = base + j
        for c in range(NCH):
            xmove(ja, c)
        e.wait_ge(sem["x1a"], 8 * (ja + 1))
        if ph == 0:
            e.dma_start(out=x1sp[j], in_=xsl(j)).then_inc(sp("x1sp", j), 16)
            if _STAGE == "A":
                e.dma_start(out=outh[j], in_=xsl(j)).then_inc(sp("outw", j), 16)
            for c in range(NCH):
                if c <= 1:
                    e.wait_ge(sem["s2mm"], 8 * (ja + 1))
                else:
                    e.wait_ge(sem["s2mmB"], 8 * j + c - 1)
                e.activation(sqsl(c), xch(j, c), AF.Square).then_inc(
                    sem["sqB"], 1)
            e.wait_ge(sem["rwB1"], j + 1)
            e.activation(rf(R_S1), rf(R_V), AF.Sqrt, bias=epsc[0:1, 0:1]).then_inc(
                sem["sqB1"], 1)
            e.wait_ge(sem["rwB2"], j + 1)
            e.activation(rf(R_S2), rf(R_S1), AF.Sqrt, bias=epsc[0:1, 0:1]).then_inc(
                sem["sqB2"], 1)
            e.wait_ge(sem["rowB"], j + 1)
            e.dma_start(out=stsp[j:j + 1, 0:1024],
                        in_=rowsB[0:1, 0:1024]).then_inc(sp("stw", j), 16)
        else:
            e.dma_start(out=outh[j][:, 0:4 * T],
                        in_=xsl(j)[:, 0:4 * T]).then_inc(sp("owA", j), 16)
            e.dma_start(out=outh[j][:, 4 * T:8 * T],
                        in_=xsl(j)[:, 4 * T:8 * T]).then_inc(sp("owB", j), 16)

    def emit_scalar(e):
        e.wait_ge(sem["cst1"], 1)
        e.wait_ge(sem["cst0"], 96)
        if not ST_A:
            return
        emit_ffn_phase_act(e, 0, NTA)
        if not ST_B:
            return
        def affB(jt, c):
            e.wait_ge(sem["h2t"], 16 * jt + 2 * (c + 1))
            e.activation(h1ch(jt, c), tscr[:, (c % 2) * T:(c % 2 + 1) * T],
                         AF.Identity, bias=bac[:, c:c + 1],
                         scale=gac[:, c:c + 1]).then_inc(sem["h2a"], 1)

        for c in range(NCH):
            affB(0, c)
        for j in range(NT):
            for p in range(NE):
                i = j * NE + p
                e.wait_ge(sem["qkvm"], i + 1)
                if p == 0 and j >= 1:
                    e.wait_ge(sem["kna"], j)
                if p == 8 and j >= 1:
                    e.wait_ge(sem["ropQ"], j)
                if p == 12:
                    e.wait_ge(sem["qnsp_"], QSPF * j)
                    if j >= 1:
                        e.wait_ge(sem["qnv"], j)
                if p == 16 and j >= 1:
                    e.wait_ge(sem["knvd"], j)
                if p < 8:
                    dst = a2K(p)
                elif p < 16:
                    dst = a2Q(p - 8)
                else:
                    dst = a2V(p - 16)
                e.activation(dst, rot_slot(i), AF.Copy).then_inc(sem["qkve"], 1)
                if 1 <= p <= 8 and j < NT - 1:
                    affB(j + 1, p - 1)
                if 2 <= p <= 9:
                    c = p - 2
                    if j == 0 and c <= 1:
                        e.wait_ge(sem["s2mmB"], 8 * NT)
                    if c >= 2:
                        e.wait_ge(sem["s2K"], 8 * j + c - 1)
                    e.activation(sqsl(c), a2K(c), AF.Square).then_inc(
                        sem["sqK"], 1)
                if 11 <= p <= 18:
                    c = p - 11
                    if c <= 1:
                        e.wait_ge(sem["s2K"], 8 * j + 7 + c)
                    else:
                        e.wait_ge(sem["s2Q"], 8 * j + c - 1)
                    e.activation(sqsl(c), a2Q(c), AF.Square).then_inc(
                        sem["sqQ"], 1)
                if p == 19:
                    e.wait_ge(sem["s2K"], 8 * (j + 1))
                    e.wait_ge(sem["s2Q"], 8 * (j + 1))
                    e.activation(rf(R_M), stp[0:1, 0:T], AF.Sqrt,
                                 bias=l2c[0:1, 0:1])
                    e.activation(rf(R_V), stp[0:1, T:2 * T], AF.Sqrt,
                                 bias=l2c[0:1, 0:1]).then_inc(sem["sqKQ"], 1)
        if not ST_D1:
            return
        for j in range(NT):
            for ec in range(NCH):
                i = j * NCH + ec
                e.wait_ge(sem["ym"], i + 1)
                if ec == 0 and j == 0:
                    e.wait_ge(sem["qnsp_"], QSPF * NT)
                if ec == 0 and j >= 2:
                    e.wait_ge(sp("ysp", j), pcnt(j - 2))
                e.activation(a2Y(j, ec), rot10(i), AF.Copy).then_inc(
                    sem["ye"], 1)
            e.dma_start(out=ysp[j], in_=a2Ysl(j)).then_inc(sp("ysp", j), 16)
            if _STAGE == "D1":
                e.dma_start(out=outh[j], in_=a2Ysl(j)).then_inc(sp("outw", j), 16)
        if not ST_D2:
            return
        emit_ffn_phase_act(e, 1, NT)

    with nc.Block() as block:
        @block.sync
        def _(eng):
            emit_sync(eng)

        @block.tensor
        def _(eng):
            emit_tensor(eng)

        @block.vector
        def _(eng):
            emit_vector(eng)

        @block.scalar
        def _(eng):
            emit_scalar(eng)

        @block.gpsimd
        def _(eng):
            emit_gpsimd(eng)

    return nc


def _perm():
    p = []
    for blk in range(2):
        base = blk * 256
        p += list(range(base, base + 256, 2))
        p += list(range(base + 1, base + 256, 2))
    p += list(range(512, 1024))
    return np.array(p, dtype=np.int64)


def _prep(inputs):
    X = np.ascontiguousarray(np.asarray(inputs["L0_attn_X"], dtype=np.float32))
    W_in = np.asarray(inputs["L0_attn_W_in"], dtype=np.float32)
    W_out = np.asarray(inputs["L0_attn_W_out"], dtype=np.float32)
    perm = _perm()
    pfull = np.concatenate([perm, 1024 + perm, 2048 + perm])
    win_prep = np.ascontiguousarray(W_in[pfull, :].T.astype(np.float32))
    wout_prep = np.ascontiguousarray(W_out[:, perm].T.astype(np.float32))

    def cols(v):
        return np.ascontiguousarray(np.asarray(v, dtype=np.float32).reshape(8, 128).T)

    b1cols = np.ascontiguousarray(
        np.asarray(inputs["L0_ffn_fc1_bias"], dtype=np.float32).reshape(32, 128).T)
    b2cols = cols(inputs["L0_ffn_fc2_bias"])
    gf = cols(inputs["L0_ffn_ln_gamma"])
    bf = cols(inputs["L0_ffn_ln_beta"])
    ga = cols(inputs["L0_attn_ln_g"])
    ba = cols(inputs["L0_attn_ln_b"])
    q1 = np.ascontiguousarray(np.asarray(inputs["L0_ffn_fc1_q"], dtype=np.int32))
    s1 = np.ascontiguousarray(np.asarray(inputs["L0_ffn_fc1_scales"], dtype=np.float32))
    q2 = np.ascontiguousarray(np.asarray(inputs["L0_ffn_fc2_q"], dtype=np.int32))
    s2 = np.ascontiguousarray(np.asarray(inputs["L0_ffn_fc2_scales"], dtype=np.float32))

    inv = (np.float32(1.0) /
           (np.float32(10000.0) **
            (np.arange(0, 512, 2, dtype=np.float32) / np.float32(512.0)))).astype(np.float32)

    in_maps = []
    for c in range(8):
        xs = X[:, c * POS:(c + 1) * POS, :]
        xs = xs.reshape(2, 8, T, NCH, 128)
        xs = np.ascontiguousarray(xs.transpose(0, 1, 4, 3, 2))
        xs = np.ascontiguousarray(xs.reshape(NT, 128, NCH * T))
        pos = np.arange(c * POS, (c + 1) * POS, dtype=np.float32).reshape(8, T)
        cst = np.empty((8, 128, 4 * T), dtype=np.float32)
        for jt in range(8):
            ang = (pos[jt][None, :] * inv[:, None]).astype(np.float32)
            cosv = np.cos(ang).astype(np.float32)
            sinv = np.sin(ang).astype(np.float32)
            cst[jt, :, 0:T] = cosv[0:128]
            cst[jt, :, T:2 * T] = cosv[128:256]
            cst[jt, :, 2 * T:3 * T] = sinv[0:128]
            cst[jt, :, 3 * T:4 * T] = sinv[128:256]
        in_maps.append({
            "xin": xs, "q1": q1, "s1": s1, "b1": b1cols,
            "q2": q2, "s2": s2, "b2": b2cols,
            "gf": gf, "bf": bf, "ga": ga, "ba": ba,
            "win": win_prep, "wout": wout_prep, "cst": cst,
        })
    return in_maps


def kernel(**inputs):
    in_maps = _prep(inputs)
    nc = _build()
    res = run_bass_kernel_spmd(nc, in_maps, core_ids=list(range(8)))
    out = np.empty((2, 16384, 1024), dtype=np.float32)
    for c in range(8):
        o = np.asarray(res.results[c]["outp"])
        o = o.reshape(2, 8, 128, NCH, T).transpose(0, 1, 4, 3, 2)
        out[:, c * POS:(c + 1) * POS, :] = o.reshape(2, POS, D)
    return out


# version 34
# speedup vs baseline: 1.0103x; 1.0103x over previous
import numpy as np
import ml_dtypes
import concourse.bass as bass
import concourse.mybir as mybir
from concourse.bass_utils import run_bass_kernel_spmd

F32 = mybir.dt.float32
BF16 = mybir.dt.bfloat16
I32 = mybir.dt.int32
AF = mybir.ActivationFunctionType
ALU = mybir.AluOpType
AX = mybir.AxisListType

D = 1024
NCH = 8
T = 256
NT = 16
POS = 2048
FF = 4096
NF = 32
EPS = 1e-5
L2EPS = 1e-6
NE = 24

_STAGE = "full"
_NTRUN = NT

DQ_UNITS = []
for _xp in range(4):
    for _g in range(8):
        DQ_UNITS.append(("w1", _g, _xp))
    for _g in range(8 * _xp, 8 * _xp + 8):
        DQ_UNITS.append(("w2", _g, 0))
N_DQ = len(DQ_UNITS)
W1_READY = {}
W2_READY = {}
for _i, (_k, _g, _x) in enumerate(DQ_UNITS):
    if _k == "w1":
        for _fs in (2 * _x, 2 * _x + 1):
            W1_READY[_fs] = max(W1_READY.get(_fs, 0), _i + 1)
    else:
        W2_READY[_g] = max(W2_READY.get(_g, 0), _i + 1)


def _build():
    NTA = _NTRUN if _STAGE in ("DQ", "A") else NT
    ST_A = _STAGE != "DQ"
    ST_B = _STAGE in ("B", "D1", "full")
    ST_D1 = _STAGE in ("D1", "full")
    ST_D2 = _STAGE == "full"
    nc = bass.Bass("TRN2", target_bir_lowering=False, debug=False,
                   detect_race_conditions=False)

    xin = nc.dram_tensor("xin", [NT, 128, NCH * T], F32, kind="ExternalInput").ap()
    q1h = nc.dram_tensor("q1", [D, FF], I32, kind="ExternalInput").ap()
    s1h = nc.dram_tensor("s1", [8, FF], F32, kind="ExternalInput").ap()
    b1h = nc.dram_tensor("b1", [128, NF], F32, kind="ExternalInput").ap()
    q2h = nc.dram_tensor("q2", [FF, D], I32, kind="ExternalInput").ap()
    s2h = nc.dram_tensor("s2", [32, D], F32, kind="ExternalInput").ap()
    b2h = nc.dram_tensor("b2", [128, NCH], F32, kind="ExternalInput").ap()
    gfh = nc.dram_tensor("gf", [128, NCH], F32, kind="ExternalInput").ap()
    bfh = nc.dram_tensor("bf", [128, NCH], F32, kind="ExternalInput").ap()
    gah = nc.dram_tensor("ga", [128, NCH], F32, kind="ExternalInput").ap()
    bah = nc.dram_tensor("ba", [128, NCH], F32, kind="ExternalInput").ap()
    winh = nc.dram_tensor("win", [D, 3 * D], F32, kind="ExternalInput").ap()
    wouth = nc.dram_tensor("wout", [D, D], F32, kind="ExternalInput").ap()
    csth = nc.dram_tensor("cst", [8, 128, 4 * T], F32, kind="ExternalInput").ap()
    outh = nc.dram_tensor("outp", [NT, 128, NCH * T], F32, kind="ExternalOutput").ap()

    x1sp = nc.dram_tensor("x1sp", [NT, 128, NCH * T], F32, kind="Internal").ap()
    qnsp = nc.dram_tensor("qnsp", [NT, 128, NCH * T], F32, kind="Internal").ap()
    ysp = nc.dram_tensor("ysp", [NT, 128, NCH * T], F32, kind="Internal").ap()
    w1bk = nc.dram_tensor("w1bk", [8, 128, FF], BF16, kind="Internal").ap()
    stsp = nc.dram_tensor("stsp", [NT, 1024], F32, kind="Internal").ap()
    arint = nc.dram_tensor("arin_", [128, 16], F32, kind="Internal")
    aroutt = nc.dram_tensor("arout_", [128, 16], F32, kind="Internal")
    arin = arint.ap()
    arout = aroutt.ap()

    arena = nc.alloc_sbuf_tensor("arena", [128, 65536], BF16)
    arena2 = nc.alloc_sbuf_tensor("arena2", [128, 7168], F32)
    xbuf = nc.alloc_sbuf_tensor("xbuf", [128, 2 * NCH * T], F32)
    h1buf = nc.alloc_sbuf_tensor("h1buf", [128, 2 * NCH * T], BF16)
    y1buf = nc.alloc_sbuf_tensor("y1buf", [128, 8 * T], BF16)
    sqt = nc.alloc_sbuf_tensor("sqt", [128, 2 * T], F32)
    tscr = nc.alloc_sbuf_tensor("tscr", [128, 2 * T], F32)
    rowsf = nc.alloc_sbuf_tensor("rowsf", [1, 2048], F32)
    rowsB = nc.alloc_sbuf_tensor("rowsB", [1, 1024], F32)
    costg = nc.alloc_sbuf_tensor("costg", [128, 4 * T], F32)
    b1c = nc.alloc_sbuf_tensor("b1c", [128, NF], F32)
    b2c = nc.alloc_sbuf_tensor("b2c", [128, NCH], F32)
    qb2c = nc.alloc_sbuf_tensor("qb2c", [128, NCH], F32)
    gfc = nc.alloc_sbuf_tensor("gfc", [128, NCH], F32)
    bfc = nc.alloc_sbuf_tensor("bfc", [128, NCH], F32)
    gac = nc.alloc_sbuf_tensor("gac", [128, NCH], F32)
    bac = nc.alloc_sbuf_tensor("bac", [128, NCH], F32)
    onesc = nc.alloc_sbuf_tensor("onesc", [128, 1], F32)
    epsc = nc.alloc_sbuf_tensor("epsc", [128, 1], F32)
    l2c = nc.alloc_sbuf_tensor("l2c", [128, 1], F32)
    onesr = nc.alloc_sbuf_tensor("onesr", [1, 128], F32)
    a2t = nc.alloc_sbuf_tensor("a2t", [128, 16], F32)
    aft = nc.alloc_sbuf_tensor("aft", [128, 16], F32)
    a2acc = nc.alloc_sbuf_tensor("a2acc", [128, 128], F32)
    bcsb = nc.alloc_sbuf_tensor("bcsb", [128, 512], F32)

    zb = [nc.alloc_psum_tensor(f"zb{i}", [128, 512], F32) for i in range(4)]
    hb = [nc.alloc_psum_tensor(f"hb{i}", [128, 512], F32) for i in range(2)]
    stp = nc.alloc_psum_tensor("stp", [1, 512], F32)
    bcp = nc.alloc_psum_tensor("bcp", [128, 512], F32)

    ROT = ([(zb[k], 0) for k in range(4)] + [(hb[0], 0)] +
           [(zb[k], 256) for k in range(4)])

    def rot_slot(i):
        t_, off = ROT[i % 9]
        return t_[:, off:off + T]

    ROT10 = ROT + [(hb[1], 0)]

    def rot10(i):
        t_, off = ROT10[i % 10]
        return t_[:, off:off + T]

    names = ["dq_v", "w1bkd", "sqA", "s2mm", "zrA", "zrB",
             "rw1", "sq1", "rw2", "sq2", "rowA", "bch", "h1t", "h1a",
             "hid", "y1", "zev", "bcx", "x1a", "sqB", "s2mmB",
             "rwB1", "sqB1", "rwB2", "sqB2", "rowB",
             "xb", "stl", "cs", "bch2", "h2t", "h2a", "qkvm", "qkve",
             "sqK", "s2K", "ropQ", "sqQ", "s2Q",
             "sqKQ", "rkq", "bckq", "qnv", "qnsp_", "arr", "cc", "af",
             "winl", "woutl", "wa", "ym", "ye",
             "w1r", "std", "bcu", "x2a", "cst1",
             "cst0", "dq_q0", "dq_q1", "dq_q2", "dq_q3",
             "dq_s0", "dq_s1", "dq_s2",
             "xa0", "xa1", "xb0", "xb1", "yd0", "yd1",
             "qn10", "qn11", "x1sp0", "x1sp1",
             "owA0", "owA1", "owB0", "owB1", "xdA0", "xdA1", "xdB0", "xdB1",
             "ysp0", "ysp1", "stw0", "stw1", "bcc", "kna", "knvd", "sqD"]
    if _STAGE in ("A", "D1", "DQ"):
        names += ["outw0", "outw1"]
    if _STAGE == "B":
        names += ["afd"]
    sem = {n: nc.alloc_semaphore(name="s_" + n) for n in names}

    def sp(base, j):
        return sem[base + str(j % 2)]

    def pcnt(j):
        return 16 * (j // 2 + 1)

    def sp4(base, j):
        return sem[base + str(j % 4)]

    QSPF = 64 if _STAGE == "B" else 32

    def xsl(j):
        return xbuf[:, (j % 2) * NCH * T:(j % 2 + 1) * NCH * T]

    def xch(j, c):
        return xbuf[:, (j % 2) * NCH * T + c * T:(j % 2) * NCH * T + (c + 1) * T]

    def h1ch(j, c):
        return h1buf[:, (j % 2) * NCH * T + c * T:(j % 2) * NCH * T + (c + 1) * T]

    def w1t(g, f):
        return arena[:, g * FF + f * 128:g * FF + (f + 1) * 128]

    def w2t(g, zc):
        return arena[:, 32768 + g * D + zc * 128:32768 + g * D + (zc + 1) * 128]

    def wint(dc, ec):
        return arena[:, dc * 3072 + ec * 128:dc * 3072 + (ec + 1) * 128]

    def qnD1(j, dc):
        return arena[:, (j % 2) * 2048 + dc * T:(j % 2) * 2048 + (dc + 1) * T]

    def a2Q(c):
        return arena2[:, c * T:(c + 1) * T]

    def a2K(c):
        return arena2[:, 2048 + c * T:2048 + (c + 1) * T]

    def a2V(c):
        return arena2[:, 4096 + c * T:4096 + (c + 1) * T]

    def a2rot(i):
        return arena2[:, 6144 + i * T:6144 + (i + 1) * T]

    def a2Y(j, c):
        return arena2[:, (j % 2) * 2048 + c * T:(j % 2) * 2048 + (c + 1) * T]

    def a2Ysl(j):
        return arena2[:, (j % 2) * 2048:(j % 2) * 2048 + 2048]

    def sbcsl(i):
        return arena2[:, (i % 3) * 1024:(i % 3) * 1024 + 1024]

    def qfsl(i):
        return arena2[:, 3072 + (i % 4) * 1024:3072 + (i % 4) * 1024 + 1024]

    ECO = list(range(8, 16)) + list(range(0, 8)) + list(range(16, 24))

    R_M, R_V, R_S1, R_S2 = 0, 256, 512, 768
    R_AH, R_BH, R_AU, R_BU = 1024, 1280, 1536, 1792
    B_AH, B_BH, B_R2, B_MR, B_EV = 0, 256, 512, 768, 1024

    def sqsl(c):
        return sqt[:, (c % 2) * T:(c % 2 + 1) * T]

    def rf(off, n=256):
        return rowsf[0:1, off:off + n]

    def rb(off, n=256):
        return rowsB[0:1, off:off + n]

    def dq_dst(kind, g, x):
        if kind == "w1":
            return arena[:, g * FF + x * 1024:g * FF + (x + 1) * 1024]
        return arena[:, 32768 + g * D:32768 + (g + 1) * D]

    def emit_sync(e):
        e.dma_start(out=b1c[:], in_=b1h[:]).then_inc(sem["cst0"], 16)
        e.dma_start(out=b2c[:], in_=b2h[:]).then_inc(sem["cst0"], 16)
        e.dma_start(out=gfc[:], in_=gfh[:]).then_inc(sem["cst0"], 16)
        e.dma_start(out=bfc[:], in_=bfh[:]).then_inc(sem["cst0"], 16)
        e.dma_start(out=gac[:], in_=gah[:]).then_inc(sem["cst0"], 16)
        e.dma_start(out=bac[:], in_=bah[:]).then_inc(sem["cst0"], 16)
        if ST_A:
            e.dma_start(out=xsl(0), in_=xin[0]).then_inc(sem["xa0"], 16)
            if NTA > 1:
                e.dma_start(out=xsl(1), in_=xin[1]).then_inc(sem["xa1"], 16)
        for i, (kind, g, x) in enumerate(DQ_UNITS):
            if i >= 3:
                e.wait_ge(sem["dq_v"], i - 2)
            if kind == "w1":
                src = s1h[g:g + 1, x * 1024:(x + 1) * 1024]
            else:
                src = s2h[g:g + 1, 0:1024]
            e.dma_start(out=sbcsl(i), in_=src.partition_broadcast(128)
                        ).then_inc(sem["dq_s" + str(i % 3)], 16)
        for j in range(2, NTA if ST_A else 0):
            e.wait_ge(sp("x1sp", j), pcnt(j - 2))
            e.wait_ge(sem["s2mmB"], 8 * (j - 1))
            if _STAGE == "A":
                e.wait_ge(sp("outw", j), pcnt(j - 2))
            e.dma_start(out=xsl(j), in_=xin[j]).then_inc(sp("xa", j), 16)
        def qn_spill(jj):
            e.wait_ge(sem["qnv"], jj + 1)
            e.dma_start(out=qnsp[jj][:, 0:1024],
                        in_=arena2[:, 6144:7168]).then_inc(sem["qnsp_"], 16)
            e.dma_start(out=qnsp[jj][:, 1024:2048],
                        in_=arena2[:, 1024:2048]).then_inc(sem["qnsp_"], 16)
            if _STAGE == "B":
                e.dma_start(out=outh[jj][:, 0:1024],
                            in_=arena2[:, 6144:7168]).then_inc(sem["qnsp_"], 16)
                e.dma_start(out=outh[jj][:, 1024:2048],
                            in_=arena2[:, 1024:2048]).then_inc(sem["qnsp_"], 16)

        for j in range(NT if ST_B else 0):
            if j == 0:
                e.wait_ge(sem["x1sp0"], 16 * (NT // 2))
                e.wait_ge(sem["s2mmB"], 8 * (NT - 1))
            elif j == 1:
                e.wait_ge(sem["x1sp1"], 16 * (NT // 2))
                e.wait_ge(sem["s2mmB"], 8 * NT)
            else:
                e.wait_ge(sem["h2t"], 16 * (j - 1))
            e.dma_start(out=xsl(j), in_=x1sp[j]).then_inc(sp("xb", j), 16)
            if j == 0:
                e.wait_ge(sem["stw0"], 16 * (NT // 2))
                e.wait_ge(sem["stw1"], 16 * (NT // 2))
            else:
                e.wait_ge(sem["bch2"], j)
            e.dma_start(out=rowsB[0:1, 0:1024],
                        in_=stsp[j:j + 1, 0:1024]).then_inc(sem["stl"], 16)
            if j >= 2:
                qn_spill(j - 2)
            e.wait_ge(sem["ropQ"], j)
            e.dma_start(out=costg[:], in_=csth[j % 8]).then_inc(sem["cs"], 16)
        if ST_B:
            qn_spill(NT - 2)
            qn_spill(NT - 1)
        for j in range(NT if ST_D2 else 0):
            if j == 0:
                e.wait_ge(sem["bch2"], NT)
            else:
                e.wait_ge(sem["bcu"], j)
            e.dma_start(out=rowsB[0:1, 0:512],
                        in_=stsp[j:j + 1, 512:1024]).then_inc(sem["std"], 16)
            if j < 2:
                e.wait_ge(sem["qnv"], NT)
            else:
                e.wait_ge(sp("owA", j), pcnt(j - 2))
            e.dma_start(out=xsl(j)[:, 0:4 * T],
                        in_=x1sp[j][:, 0:4 * T]).then_inc(sp("xdA", j), 16)
            if j >= 2:
                e.wait_ge(sp("owB", j), pcnt(j - 2))
            e.dma_start(out=xsl(j)[:, 4 * T:8 * T],
                        in_=x1sp[j][:, 4 * T:8 * T]).then_inc(sp("xdB", j), 16)
            e.wait_ge(sem["ysp0"], 16 * (NT // 2))
            e.wait_ge(sem["ysp1"], 16 * (NT // 2))
            if j >= 2:
                e.wait_ge(sem["x2a"], 8 * (j - 1))
            e.dma_start(out=a2Ysl(j), in_=ysp[j]).then_inc(sp("yd", j), 16)
        if _STAGE == "B":
            e.wait_ge(sem["qnsp_"], QSPF * NT)
            e.wait_ge(sem["afd"], 32)
        elif _STAGE == "DQ":
            e.wait_ge(sem["outw0"], 16)
        elif _STAGE in ("A", "D1"):
            nw = NTA if _STAGE == "A" else NT
            e.wait_ge(sem["outw0"], 16 * ((nw + 1) // 2))
            if nw > 1:
                e.wait_ge(sem["outw1"], 16 * (nw // 2))
        else:
            for s in ("owA0", "owA1", "owB0", "owB1"):
                e.wait_ge(sem[s], 16 * (NT // 2))

    def emit_gpsimd(e):
        for i, (kind, g, x) in enumerate(DQ_UNITS):
            if i >= 4:
                e.wait_ge(sem["dq_v"], i - 3)
            if kind == "w1":
                src = q1h[g * 128:(g + 1) * 128, x * 1024:(x + 1) * 1024]
            else:
                src = q2h[g * 128:(g + 1) * 128, 0:1024]
            e.dma_start(out=qfsl(i), in_=src).then_inc(
                sem["dq_q" + str(i % 4)], 16)
        e.wait_ge(sem["dq_v"], N_DQ)
        for g in range(8):
            e.dma_start(out=w1bk[g], in_=arena[:, g * FF:(g + 1) * FF]
                        ).then_inc(sem["w1bkd"], 16)
        if _STAGE == "DQ":
            e.wait_ge(sem["w1bkd"], 16 * 8)
            e.dma_start(out=outh[0][:, 0:2048],
                        in_=arena[:, 0:2048]).then_inc(sem["outw0"], 16)
            return
        if not ST_B:
            return
        e.wait_ge(sem["zrB"], 32 * NT)
        for dc in range(NCH):
            e.dma_start(out=arena[:, dc * 3072:(dc + 1) * 3072],
                        in_=winh[dc * 128:(dc + 1) * 128, :]).then_inc(sem["winl"], 16)
        for dc in range(NCH):
            e.dma_start(out=arena[:, 24576 + dc * D:24576 + (dc + 1) * D],
                        in_=wouth[dc * 128:(dc + 1) * 128, :]).then_inc(sem["woutl"], 16)
        e.wait_ge(sem["qnv"], NT + 1)
        e.dma_start(out=arin[:], in_=a2t[:]).then_inc(sem["arr"], 16)
        e.wait_ge(sem["arr"], 16)
        e.collective_compute(
            "AllReduce", ALU.add,
            ins=[arint.ap().opt()], outs=[aroutt.ap().opt()],
            replica_groups=[[0, 1, 2, 3, 4, 5, 6, 7]],
        ).then_inc(sem["cc"])
        e.wait_ge(sem["cc"], 1)
        e.dma_start(out=aft[:], in_=arout[:]).then_inc(sem["af"], 16)
        if _STAGE == "B":
            e.wait_ge(sem["qnsp_"], QSPF * NT)
            e.dma_start(out=outh[0][:, 2032:2048], in_=aft[:]).then_inc(sem["afd"], 16)
            e.dma_start(out=outh[1][:, 2032:2048], in_=arin[:]).then_inc(sem["afd"], 16)
        if not ST_D1:
            return
        for j in range(NT):
            if j == 0:
                e.wait_ge(sem["qnsp_"], QSPF * NT)
            if j >= 2:
                e.wait_ge(sem["ym"], 8 * (j - 1))
            e.dma_start(out=arena[:, (j % 2) * 2048:(j % 2) * 2048 + 2048],
                        in_=qnsp[j]).then_inc(sp("qn1", j), 16)
        e.wait_ge(sem["ym"], 8 * NT)
        for g in range(8):
            e.dma_start(out=arena[:, g * FF:(g + 1) * FF],
                        in_=w1bk[g]).then_inc(sem["w1r"], 16)
        if ST_D2:
            for jt in range(NT):
                for c in range(NCH):
                    e.wait_ge(sem["x2a"], 8 * jt + c + 1)
                    if c >= 2:
                        e.wait_ge(sem["s2mm"], 8 * (NT + jt) + c - 1)
                    elif jt >= 1:
                        e.wait_ge(sem["s2mm"], 8 * (NT + jt))
                    e.tensor_mul(sqsl(c), xch(jt, c), xch(jt, c)).then_inc(
                        sem["sqD"], 1)

    def _z_half(e, gi, half):
        fi = gi % NF
        if fi == 0 and gi >= NF:
            e.wait_ge(sem["zev"], 8 * (gi // NF - 1) + 4 * (half + 1))
        e.wait_ge(sem["y1"], gi + 1)
        buf = y1buf[:, (gi % 8) * T:(gi % 8 + 1) * T]
        ins = None
        for z in range(4 * half, 4 * half + 4):
            bank, off = z // 2, (z % 2) * T
            ins = e.matmul(zb[bank][:, off:off + T], w2t(fi, z), buf,
                           start=(fi == 0 and z % 2 == 0),
                           stop=(fi == NF - 1),
                           skip_group_check=True)
        ins.then_inc(sem["zrA" if half == 0 else "zrB"], 1)

    def _pe_s1(e, jslot):
        for c in range(NCH):
            e.matmul(stp[0:1, 0:T], onesc[:, 0:1], xch(jslot, c),
                     start=(c == 0), stop=(c == NCH - 1))

    def _pe_s2(e, c, count_sem):
        ins = e.matmul(stp[0:1, T:2 * T], onesc[:, 0:1], sqsl(c),
                       start=False, stop=(c == NCH - 1),
                       skip_group_check=True)
        ins.then_inc(sem[count_sem], 1)

    def emit_ffn_phase_pe(e, ph, ntiles):
        base = ph * NT
        if ph == 0:
            e.wait_ge(sem["xa0"], 16)
            _pe_s1(e, 0)
            for c in range(NCH):
                e.wait_ge(sem["sqA"], c + 1)
                _pe_s2(e, c, "s2mm")
            e.wait_ge(sem["rowA"], 1)
            e.matmul(bcp[:, 0:512], onesr[0:1, :], rf(R_AH, 512),
                     start=True, stop=True).then_inc(sem["bch"], 1)
        else:
            e.wait_ge(sem["std"], 16)
            e.matmul(bcp[:, 0:512], onesr[0:1, :], rb(0, 512),
                     start=True, stop=True).then_inc(sem["bcu"], 1)
            e.wait_ge(sem["x2a"], 8)
            _pe_s1(e, 0)
            for c in range(NCH):
                e.wait_ge(sem["sqD"], c + 1)
                _pe_s2(e, c, "s2mm")
            e.wait_ge(sem["rowA"], NT + 1)
            e.matmul(bcp[:, 0:512], onesr[0:1, :], rf(R_AH, 512),
                     start=True, stop=True).then_inc(sem["bch"], 1)
        for j in range(ntiles):
            ja = base + j
            last = (j == ntiles - 1)
            for f in range(NF):
                gi = ja * NF + f
                if f == 0:
                    e.wait_ge(sem["h1a"], 8 * (ja + 1))
                    if ph == 1 and j == 0:
                        e.wait_ge(sem["w1r"], 16 * 8)
                if ph == 0 and j == 0 and f % 4 == 0:
                    e.wait_ge(sem["dq_v"], W1_READY[f // 4])
                if gi >= 2:
                    e.wait_ge(sem["y1"], gi - 1)
                ins = None
                for k in range(NCH):
                    ins = e.matmul(hb[f % 2][:, 0:T], w1t(k, f), h1ch(j, k),
                                   start=(k == 0), stop=(k == NCH - 1))
                ins.then_inc(sem["hid"], 1)
                if f >= 4:
                    if ph == 0 and j == 0:
                        e.wait_ge(sem["dq_v"], W2_READY[(gi - 4) % NF])
                    _z_half(e, gi - 4, 0)
                if f >= 7:
                    _z_half(e, gi - 7, 1)
                if ph == 0:
                    if f == 2:
                        e.wait_ge(sem["h1t"], 16 * (ja + 1))
                        e.wait_ge(sem["rowA"], ja + 1)
                        e.matmul(bcp[:, 0:512], onesr[0:1, :], rf(R_AU, 512),
                                 start=True, stop=True).then_inc(sem["bcx"], 1)
                    if 1 <= f <= 8 and j >= 1:
                        c = f - 1
                        e.wait_ge(sem["x1a"], 8 * (ja - 1) + c + 1)
                        if c == 0:
                            e.wait_ge(sem["rw1"], ja + 1)
                        e.matmul(stp[0:1, 0:T], onesc[:, 0:1], xch(j - 1, c),
                                 start=(c == 0), stop=(c == NCH - 1))
                    if 3 <= f <= 10 and j >= 1:
                        c = f - 3
                        e.wait_ge(sem["sqB"], 8 * (j - 1) + c + 1)
                        _pe_s2(e, c, "s2mmB")
                    if f == 13 and not last:
                        e.wait_ge(sp("xa", j + 1), pcnt(j + 1))
                        e.wait_ge(sem["rwB1"], j)
                        e.wait_ge(sem["rw1"], ja + 1)
                        _pe_s1(e, j + 1)
                    if 15 <= f <= 22 and not last:
                        c = f - 15
                        e.wait_ge(sem["sqA"], 8 * (ja + 1) + c + 1)
                        _pe_s2(e, c, "s2mm")
                    if f == 27 and not last:
                        e.wait_ge(sem["rowA"], ja + 2)
                        e.wait_ge(sem["bcc"], ja + 1)
                        e.matmul(bcp[:, 0:512], onesr[0:1, :], rf(R_AH, 512),
                                 start=True, stop=True).then_inc(sem["bch"], 1)
                else:
                    if f == 2 and not last:
                        e.wait_ge(sem["std"], 16 * (j + 2))
                        e.wait_ge(sem["h1t"], 16 * (ja + 1))
                        e.matmul(bcp[:, 0:512], onesr[0:1, :], rb(0, 512),
                                 start=True, stop=True).then_inc(sem["bcu"], 1)
                    if 16 <= f <= 23 and not last:
                        c = f - 16
                        e.wait_ge(sem["x2a"], 8 * (j + 1) + c + 1)
                        if c == 0:
                            e.wait_ge(sem["rw1"], ja + 1)
                        e.matmul(stp[0:1, 0:T], onesc[:, 0:1], xch(j + 1, c),
                                 start=(c == 0), stop=(c == NCH - 1))
                    if f == 20:
                        if not last:
                            e.wait_ge(sem["x2a"], 8 * (j + 2))
                        e.wait_ge(sem["h1t"], 16 * (ja + 1))
                        e.wait_ge(sem["rowA"], ja + 1)
                        e.matmul(bcp[:, 0:512], onesr[0:1, :], rf(R_AU, 512),
                                 start=True, stop=True).then_inc(sem["bcx"], 1)
                    if 18 <= f <= 25 and not last:
                        c = f - 18
                        e.wait_ge(sem["sqD"], 8 * (j + 1) + c + 1)
                        _pe_s2(e, c, "s2mm")
                    if f == 29 and not last:
                        e.wait_ge(sem["rowA"], ja + 2)
                        e.wait_ge(sem["bcc"], ja + 1)
                        e.matmul(bcp[:, 0:512], onesr[0:1, :], rf(R_AH, 512),
                                 start=True, stop=True).then_inc(sem["bch"], 1)
            for d in range(4):
                if ph == 0 and j == 0:
                    e.wait_ge(sem["dq_v"], W2_READY[28 + d])
                _z_half(e, ja * NF + 28 + d, 0)
                _z_half(e, ja * NF + 25 + d, 1)
            for d in range(3):
                _z_half(e, ja * NF + 29 + d, 1)
        if ph == 0:
            j = ntiles - 1
            e.wait_ge(sem["x1a"], 8 * (base + ntiles))
            e.wait_ge(sem["rw1"], base + ntiles)
            _pe_s1(e, j)
            for c in range(NCH):
                e.wait_ge(sem["sqB"], 8 * j + c + 1)
                _pe_s2(e, c, "s2mmB")

    def emit_tensor(e):
        e.wait_ge(sem["cst1"], 1)
        if not ST_A:
            return
        emit_ffn_phase_pe(e, 0, NTA)
        if not ST_B:
            return
        e.wait_ge(sem["stl"], 16)
        e.wait_ge(sem["bcc"], NT)
        e.matmul(bcp[:, 0:512], onesr[0:1, :], rb(B_AH, 512),
                 start=True, stop=True).then_inc(sem["bch2"], 1)
        for j in range(NT):
            for p, ec in enumerate(ECO):
                i = j * NE + p
                if i % 9 <= 3:
                    if i >= 4:
                        e.wait_ge(sem["qkve"], i - 3)
                elif i >= 5:
                    e.wait_ge(sem["qkve"], i - 4)
                if p == 0:
                    e.wait_ge(sem["h2a"], 8 * (j + 1))
                    if j == 0:
                        e.wait_ge(sem["winl"], 16 * NCH)
                        e.wait_ge(sem["zev"], 8 * NT)
                        e.wait_ge(sem["y1"], 32 * NT)
                ins = None
                for dc in range(NCH):
                    ins = e.matmul(rot_slot(i), wint(dc, ec), h1ch(j, dc),
                                   start=(dc == 0), stop=(dc == NCH - 1))
                ins.then_inc(sem["qkvm"], 1)
                if p == 1 and j < NT - 1:
                    e.wait_ge(sem["stl"], 16 * (j + 2))
                    e.wait_ge(sem["h2t"], 16 * (j + 1))
                    e.matmul(bcp[:, 0:512], onesr[0:1, :], rb(B_AH, 512),
                             start=True, stop=True).then_inc(sem["bch2"], 1)
                if 4 <= p <= 11:
                    c = p - 4
                    if c == 0:
                        e.wait_ge(sem["sqKQ"], j)
                    e.wait_ge(sem["sqK"], 8 * j + c + 1)
                    ins2 = e.matmul(stp[0:1, 0:T], onesc[:, 0:1], sqsl(c),
                                    start=(c == 0), stop=(c == NCH - 1))
                    ins2.then_inc(sem["s2K"], 1)
                if 12 <= p <= 19:
                    c = p - 12
                    e.wait_ge(sem["sqQ"], 8 * j + c + 1)
                    ins2 = e.matmul(stp[0:1, T:2 * T], onesc[:, 0:1], sqsl(c),
                                    start=(c == 0), stop=(c == NCH - 1))
                    ins2.then_inc(sem["s2Q"], 1)
                if p == 22:
                    e.wait_ge(sem["rkq"], j + 1)
                    if j >= 1:
                        e.wait_ge(sem["knvd"], j)
                    e.matmul(hb[1][:, 0:512], onesr[0:1, :], rf(R_S1, 512),
                             start=True, stop=True).then_inc(sem["bckq"], 1)
        if not ST_D1:
            return
        for j in range(NT):
            e.wait_ge(sp("qn1", j), pcnt(j))
            if j == 0:
                e.wait_ge(sem["wa"], 1)
            if j == 8:
                e.wait_ge(sem["wa"], 2)
            base = 8192 if j < 8 else 24576
            for ec in range(NCH):
                i = j * NCH + ec
                if i >= 5:
                    e.wait_ge(sem["ye"], i - 4)
                ins = None
                for dc in range(NCH):
                    ins = e.matmul(
                        rot10(i),
                        arena[:, base + dc * D + ec * 128:base + dc * D + (ec + 1) * 128],
                        qnD1(j, dc), start=(dc == 0), stop=(dc == NCH - 1))
                ins.then_inc(sem["ym"], 1)
        if not ST_D2:
            return
        emit_ffn_phase_pe(e, 1, NT)

    def _achain(e, ja):
        e.wait_ge(sem["s2mm"], 8 * (ja + 1))
        e.tensor_scalar_mul(rf(R_M), stp[0:1, 0:T], 1.0 / D)
        e.tensor_mul(rf(R_S1), rf(R_M), rf(R_M))
        e.tensor_scalar_mul(rf(R_V), stp[0:1, T:2 * T], 1.0 / D)
        e.tensor_sub(rf(R_V), rf(R_V), rf(R_S1)).then_inc(sem["rw1"], 1)
        e.wait_ge(sem["sq1"], ja + 1)
        e.reciprocal(rf(R_S2), rf(R_S1))
        e.tensor_mul(rf(R_S1), rf(R_S2), rf(R_S2))
        e.tensor_mul(rf(R_S1), rf(R_V), rf(R_S1)).then_inc(sem["rw2"], 1)
        e.wait_ge(sem["sq2"], ja + 1)
        e.reciprocal(rf(R_S1), rf(R_V))
        e.tensor_mul(rf(R_AH), rf(R_S2), rf(R_S1))
        e.scalar_tensor_tensor(rf(R_BH), rf(R_M), -1.0, rf(R_AH),
                               ALU.mult, ALU.mult)
        e.tensor_scalar(rf(R_AU), rf(R_S2), 0.5, 1.0, ALU.mult, ALU.add)
        e.scalar_tensor_tensor(rf(R_BU), rf(R_M), -0.5, rf(R_S2),
                               ALU.mult, ALU.mult).then_inc(sem["rowA"], 1)

    def _bchain(e, j):
        e.wait_ge(sem["s2mmB"], 8 * (j + 1))
        if j >= 1:
            e.wait_ge(sp("stw", j - 1), pcnt(j - 1))
        e.tensor_scalar_mul(rf(R_M), stp[0:1, 0:T], 1.0 / D)
        e.tensor_mul(rf(R_S1), rf(R_M), rf(R_M))
        e.tensor_scalar_mul(rf(R_V), stp[0:1, T:2 * T], 1.0 / D)
        e.tensor_sub(rf(R_V), rf(R_V), rf(R_S1)).then_inc(sem["rwB1"], 1)
        e.wait_ge(sem["sqB1"], j + 1)
        e.reciprocal(rb(B_R2), rf(R_S1))
        e.tensor_mul(rf(R_S1), rb(B_R2), rb(B_R2))
        e.tensor_mul(rf(R_S1), rf(R_V), rf(R_S1)).then_inc(sem["rwB2"], 1)
        e.wait_ge(sem["sqB2"], j + 1)
        e.reciprocal(rf(R_V), rf(R_S2))
        e.tensor_mul(rb(B_AH), rb(B_R2), rf(R_V))
        e.scalar_tensor_tensor(rb(B_BH), rf(R_M), -1.0, rb(B_AH),
                               ALU.mult, ALU.mult)
        e.tensor_mul(rb(B_MR), rf(R_M), rb(B_R2))
        e.tensor_scalar(rb(B_R2), rb(B_R2), 1.0, 1.0, ALU.mult,
                        ALU.add).then_inc(sem["rowB"], 1)

    def _h_assembly(e, ja, hsem_t, hsem_a):
        for c in range(NCH):
            if c >= 2:
                e.wait_ge(sem[hsem_a], 8 * ja + c - 1)
            sl = tscr[:, (c % 2) * T:(c % 2 + 1) * T]
            e.tensor_mul(sl, xch(ja % NT, c), bcp[:, 0:T])
            e.tensor_add(sl, sl, bcp[:, T:2 * T]).then_inc(sem[hsem_t], 2)

    def _x_assembly(e, ja, last=False):
        e.wait_ge(sem["bcc"], ja + 1)
        e.wait_ge(sem["h1a"], 8 * (ja + 1 if last else ja + 2))
        e.wait_ge(sem["zrB"], 32 * (ja + 1))
        for c in range(NCH):
            if c >= 2:
                e.wait_ge(sem["x1a"], 8 * ja + c - 1)
            sl = tscr[:, (c % 2) * T:(c % 2 + 1) * T]
            bank, off = c // 2, (c % 2) * T
            e.tensor_mul(sl, xch(ja % NT, c), bcsb[:, 0:T])
            e.tensor_add(sl, sl, bcsb[:, T:2 * T])
            e.scalar_tensor_tensor(sl, zb[bank][:, off:off + T], 0.25, sl,
                                   ALU.mult, ALU.add).then_inc(sem["zev"], 1)

    def _bc_copy(e, ja):
        e.wait_ge(sem["bcx"], ja + 1)
        e.tensor_copy(bcsb[:, 0:T], bcp[:, 0:T])
        e.tensor_copy(bcsb[:, T:2 * T], bcp[:, T:2 * T]).then_inc(sem["bcc"], 1)

    def _x2_assembly(e, jt):
        e.wait_ge(sem["bcu"], jt + 1)
        e.wait_ge(sp("yd", jt), pcnt(jt))
        e.wait_ge(sem["h1a"], 8 * (NT + jt))
        if jt >= 1:
            e.wait_ge(sem["x1a"], 8 * (NT + jt - 1))
        for c in range(NCH):
            if c == 0:
                e.wait_ge(sp("xdA", jt), pcnt(jt))
            if c == 4:
                e.wait_ge(sp("xdB", jt), pcnt(jt))
            u = tscr[:, (c % 2) * T:(c % 2 + 1) * T]
            e.tensor_mul(u, xch(jt, c), bcp[:, 0:T])
            e.tensor_sub(u, u, bcp[:, T:2 * T])
            e.tensor_add(xch(jt, c), u, a2Y(jt, c)).then_inc(sem["x2a"], 1)

    def emit_ffn_phase_v(e, ph, ntiles, skip_prologue=False):
        base = ph * NT
        if ph == 0:
            if not skip_prologue:
                _achain(e, 0)
                e.wait_ge(sem["bch"], 1)
                _h_assembly(e, 0, "h1t", "h1a")
        else:
            _x2_assembly(e, 0)
            _achain(e, NT)
            e.wait_ge(sem["bch"], NT + 1)
            _h_assembly(e, NT, "h1t", "h1a")
        for j in range(ntiles):
            ja = base + j
            last = (j == ntiles - 1)
            if j >= 1:
                _x_assembly(e, ja - 1)
            if ph == 0:
                _bc_copy(e, ja)
                if j >= 1:
                    _bchain(e, j - 1)
                if not last:
                    _achain(e, ja + 1)
                    e.wait_ge(sem["bch"], ja + 2)
                    _h_assembly(e, ja + 1, "h1t", "h1a")
            else:
                if not last:
                    _x2_assembly(e, j + 1)
                    _bc_copy(e, ja)
                    _achain(e, ja + 1)
                    e.wait_ge(sem["bch"], ja + 2)
                    _h_assembly(e, ja + 1, "h1t", "h1a")
                else:
                    _bc_copy(e, ja)
        _x_assembly(e, base + ntiles - 1, last=True)
        if ph == 0:
            _bchain(e, ntiles - 1)

    def emit_vector(e):
        e.memset(onesc[:], 1.0)
        e.memset(onesr[:], 1.0)
        e.memset(epsc[:], EPS)
        e.memset(l2c[:], L2EPS)
        e.memset(a2t[:], 0.0).then_inc(sem["cst1"], 1)
        e.wait_ge(sem["cst0"], 96)
        e.tensor_scalar_mul(qb2c[:], b2c[:], 0.25)

        def dq_unit(i):
            kind, g, x = DQ_UNITS[i]
            e.wait_ge(sem["dq_q" + str(i % 4)], 16 * (i // 4 + 1))
            e.wait_ge(sem["dq_s" + str(i % 3)], 16 * (i // 3 + 1))
            e.scalar_tensor_tensor(dq_dst(kind, g, x), qfsl(i), -128.0, sbcsl(i),
                                   ALU.add, ALU.mult).then_inc(sem["dq_v"], 1)

        for i in range(8):
            dq_unit(i)
        if ST_A:
            _achain(e, 0)
            e.wait_ge(sem["bch"], 1)
            _h_assembly(e, 0, "h1t", "h1a")
        for i in range(8, N_DQ):
            dq_unit(i)
        if not ST_A:
            return
        emit_ffn_phase_v(e, 0, NTA, skip_prologue=True)
        if not ST_B:
            return
        def qn_mul(jt):
            e.wait_ge(sem["bckq"], jt + 1)
            for i in range(4):
                e.tensor_mul(a2rot(i), a2rot(i), hb[1][:, T:2 * T])
            ins = None
            for c in range(4, NCH):
                ins = e.tensor_mul(a2Q(c), a2Q(c), hb[1][:, T:2 * T])
            ins.then_inc(sem["qnv"], 1)

        def kv_rk(jt):
            ins = None
            for c in range(NCH):
                e.tensor_mul(a2V(c), a2V(c), hb[1][:, 0:T])
                ins = e.reduce_sum(a2acc[:, jt * 8 + c:jt * 8 + c + 1],
                                   a2V(c), axis=AX.X)
            ins.then_inc(sem["knvd"], 1)

        e.wait_ge(sp("xb", 0), pcnt(0))
        e.wait_ge(sem["bch2"], 1)
        _h_assembly(e, 0, "h2t", "h2a")
        for j in range(NT):
            if j < NT - 1:
                e.wait_ge(sp("xb", j + 1), pcnt(j + 1))
                e.wait_ge(sem["bch2"], j + 2)
                e.wait_ge(sem["h2a"], 8 * (j + 1))
                _h_assembly(e, j + 1, "h2t", "h2a")
            if j >= 1:
                qn_mul(j - 1)
                kv_rk(j - 1)
            e.wait_ge(sem["cs"], 16 * (j + 1))
            e.wait_ge(sem["qnsp_"], QSPF * j)
            e.wait_ge(sem["qkve"], NE * j + 8)
            e.wait_ge(sem["sqK"], 8 * j + 4)
            for pc in range(2):
                ke, ko = a2K(2 * pc), a2K(2 * pc + 1)
                cosr = costg[:, pc * T:(pc + 1) * T]
                sinr = costg[:, 512 + pc * T:512 + (pc + 1) * T]
                e.tensor_mul(a2rot(0), ke, cosr)
                e.tensor_mul(a2rot(1), ke, sinr)
                e.tensor_mul(a2rot(2), ko, cosr)
                e.tensor_mul(a2rot(3), ko, sinr)
                e.tensor_sub(ke, a2rot(0), a2rot(3))
                e.tensor_add(ko, a2rot(2), a2rot(1))
            e.wait_ge(sem["qkve"], NE * j + 16)
            if j < NT - 1:
                e.wait_ge(sem["h2a"], 8 * (j + 2))
            ins = None
            for pc in range(2):
                qe, qo = a2Q(2 * pc), a2Q(2 * pc + 1)
                cosr = costg[:, pc * T:(pc + 1) * T]
                sinr = costg[:, 512 + pc * T:512 + (pc + 1) * T]
                e.tensor_mul(a2rot(2 * pc), qe, cosr)
                e.tensor_mul(tscr[:, 0:T], qo, sinr)
                e.tensor_sub(a2rot(2 * pc), a2rot(2 * pc), tscr[:, 0:T])
                e.tensor_mul(a2rot(2 * pc + 1), qo, cosr)
                e.tensor_mul(tscr[:, 0:T], qe, sinr)
                ins = e.tensor_add(a2rot(2 * pc + 1), a2rot(2 * pc + 1),
                                   tscr[:, 0:T])
            ins.then_inc(sem["ropQ"], 1)
            e.wait_ge(sem["sqKQ"], j + 1)
            e.reciprocal(rf(R_S1), rf(R_M))
            e.reciprocal(rf(R_S2), rf(R_V)).then_inc(sem["rkq"], 1)
            e.wait_ge(sem["qkve"], NE * (j + 1))
            ins = None
            for c in range(NCH):
                ins = e.tensor_mul(a2V(c), a2V(c), a2K(c))
            ins.then_inc(sem["kna"], 1)
        qn_mul(NT - 1)
        kv_rk(NT - 1)
        ins = None
        for b in range(2):
            for c in range(NCH):
                strided = a2acc.rearrange("p (t e) -> p t e", e=8)[:, b * 8:(b + 1) * 8, c]
                ins = e.reduce_sum(a2t[:, b * 8 + c:b * 8 + c + 1], strided, axis=AX.X)
        ins.then_inc(sem["qnv"], 1)
        if not ST_D1:
            return
        e.wait_ge(sem["af"], 16)
        e.wait_ge(sem["woutl"], 16 * NCH)
        ins = None
        for dc in range(NCH):
            ins = e.tensor_scalar_mul(
                arena[:, 8192 + dc * D:8192 + (dc + 1) * D],
                arena[:, 24576 + dc * D:24576 + (dc + 1) * D],
                aft[:, dc:dc + 1])
        ins.then_inc(sem["wa"], 1)
        ins = None
        for dc in range(NCH):
            ins = e.tensor_scalar_mul(
                arena[:, 24576 + dc * D:24576 + (dc + 1) * D],
                arena[:, 24576 + dc * D:24576 + (dc + 1) * D],
                aft[:, 8 + dc:8 + dc + 1])
        ins.then_inc(sem["wa"], 1)
        if not ST_D2:
            return
        emit_ffn_phase_v(e, 1, NT)

    def emit_ffn_phase_act(e, ph, ntiles):
        base = ph * NT

        def affine(jt, c):
            e.wait_ge(sem["h1t"], 16 * (base + jt) + 2 * (c + 1))
            e.activation(h1ch(jt, c), tscr[:, (c % 2) * T:(c % 2 + 1) * T],
                         AF.Identity, bias=bfc[:, c:c + 1],
                         scale=gfc[:, c:c + 1]).then_inc(sem["h1a"], 1)

        def xmove(jt, c):
            e.wait_ge(sem["zev"], 8 * jt + c + 1)
            e.activation(xch(jt % NT, c), tscr[:, (c % 2) * T:(c % 2 + 1) * T],
                         AF.Identity, bias=qb2c[:, c:c + 1]).then_inc(
                sem["x1a"], 1)

        if ph == 0:
            e.wait_ge(sem["xa0"], 16)
            for c in range(NCH):
                if c >= 2:
                    e.wait_ge(sem["s2mm"], c - 1)
                e.activation(sqsl(c), xch(0, c), AF.Square).then_inc(
                    sem["sqA"], 1)
            e.wait_ge(sem["rw1"], 1)
            e.activation(rf(R_S1), rf(R_V), AF.Sqrt, bias=epsc[0:1, 0:1]).then_inc(
                sem["sq1"], 1)
            e.wait_ge(sem["rw2"], 1)
            e.activation(rf(R_V), rf(R_S1), AF.Sqrt, bias=epsc[0:1, 0:1]).then_inc(
                sem["sq2"], 1)
            for c in range(NCH):
                affine(0, c)
        else:
            e.wait_ge(sem["rw1"], NT + 1)
            e.activation(rf(R_S1), rf(R_V), AF.Sqrt, bias=epsc[0:1, 0:1]).then_inc(
                sem["sq1"], 1)
            e.wait_ge(sem["rw2"], NT + 1)
            e.activation(rf(R_V), rf(R_S1), AF.Sqrt, bias=epsc[0:1, 0:1]).then_inc(
                sem["sq2"], 1)
            for c in range(NCH):
                affine(0, c)
        for j in range(ntiles):
            ja = base + j
            last = (j == ntiles - 1)
            for f in range(NF):
                gi = ja * NF + f
                e.wait_ge(sem["hid"], gi + 1)
                if gi >= 8:
                    e.wait_ge(sem["zrB"], gi - 7)
                e.activation(y1buf[:, (gi % 8) * T:(gi % 8 + 1) * T],
                             hb[f % 2][:, 0:T], AF.Silu,
                             bias=b1c[:, f:f + 1]).then_inc(sem["y1"], 1)
                if f <= 7 and j >= 1:
                    xmove(ja - 1, f)
                if ph == 0:
                    if 2 <= f <= 9 and j >= 1:
                        c = f - 2
                        if c == 0:
                            e.wait_ge(sem["x1a"], 8 * (ja - 1) + 1)
                            e.wait_ge(sem["s2mm"], 8 * (ja + 1))
                        elif c == 1:
                            e.wait_ge(sem["x1a"], 8 * (ja - 1) + c + 1)
                            e.wait_ge(sem["s2mm"], 8 * (ja + 1))
                        else:
                            e.wait_ge(sem["x1a"], 8 * (ja - 1) + c + 1)
                            e.wait_ge(sem["s2mmB"], 8 * (j - 1) + c - 1)
                        e.activation(sqsl(c), xch(j - 1, c), AF.Square
                                     ).then_inc(sem["sqB"], 1)
                    if f == 10 and j >= 1:
                        e.wait_ge(sem["rwB1"], j)
                        e.activation(rf(R_S1), rf(R_V), AF.Sqrt,
                                     bias=epsc[0:1, 0:1]).then_inc(sem["sqB1"], 1)
                    if f == 11 and j >= 1:
                        e.wait_ge(sem["rwB2"], j)
                        e.activation(rf(R_S2), rf(R_S1), AF.Sqrt,
                                     bias=epsc[0:1, 0:1]).then_inc(sem["sqB2"], 1)
                    if f == 12 and j >= 1:
                        e.wait_ge(sem["rowB"], j)
                        e.dma_start(out=stsp[j - 1:j, 0:1024],
                                    in_=rowsB[0:1, 0:1024]).then_inc(
                            sp("stw", j - 1), 16)
                    if f == 8 and j >= 1:
                        e.wait_ge(sem["x1a"], 8 * ja)
                        e.dma_start(out=x1sp[j - 1], in_=xsl(j - 1)).then_inc(
                            sp("x1sp", j - 1), 16)
                        if _STAGE == "A":
                            e.dma_start(out=outh[j - 1], in_=xsl(j - 1)).then_inc(
                                sp("outw", j - 1), 16)
                    if 14 <= f <= 21 and not last:
                        c = f - 14
                        if c <= 1:
                            if c == 0:
                                e.wait_ge(sp("xa", j + 1), pcnt(j + 1))
                            if j >= 1:
                                e.wait_ge(sem["s2mmB"], 8 * j)
                            else:
                                e.wait_ge(sem["s2mm"], 8 * (ja + 1))
                        else:
                            e.wait_ge(sem["s2mm"], 8 * (ja + 1) + c - 1)
                        e.activation(sqsl(c), xch(j + 1, c), AF.Square
                                     ).then_inc(sem["sqA"], 1)
                    if f == 23 and not last:
                        e.wait_ge(sem["rw1"], ja + 2)
                        e.activation(rf(R_S1), rf(R_V), AF.Sqrt,
                                     bias=epsc[0:1, 0:1]).then_inc(sem["sq1"], 1)
                    if f == 24 and not last:
                        e.wait_ge(sem["rw2"], ja + 2)
                        e.activation(rf(R_V), rf(R_S1), AF.Sqrt,
                                     bias=epsc[0:1, 0:1]).then_inc(sem["sq2"], 1)
                    if 28 <= f <= 31 and not last:
                        affine(j + 1, f - 28)
                else:
                    if f == 4 and j >= 1:
                        e.wait_ge(sem["x1a"], 8 * (ja - 1) + 4)
                        e.dma_start(out=outh[j - 1][:, 0:4 * T],
                                    in_=xsl(j - 1)[:, 0:4 * T]).then_inc(
                            sp("owA", j - 1), 16)
                    if f == 8 and j >= 1:
                        e.wait_ge(sem["x1a"], 8 * ja)
                        e.dma_start(out=outh[j - 1][:, 4 * T:8 * T],
                                    in_=xsl(j - 1)[:, 4 * T:8 * T]).then_inc(
                            sp("owB", j - 1), 16)
                    if 17 <= f <= 24 and not last:
                        c = f - 17
                        e.wait_ge(sem["x2a"], 8 * (j + 1) + c + 1)
                        if c <= 1:
                            e.wait_ge(sem["s2mm"], 8 * (ja + 1))
                        else:
                            e.wait_ge(sem["s2mm"], 8 * (ja + 1) + c - 1)
                        e.activation(sqsl(c), xch(j + 1, c), AF.Square
                                     ).then_inc(sem["sqA"], 1)
                    if f == 26 and not last:
                        e.wait_ge(sem["rw1"], ja + 2)
                        e.activation(rf(R_S1), rf(R_V), AF.Sqrt,
                                     bias=epsc[0:1, 0:1]).then_inc(sem["sq1"], 1)
                    if f == 27 and not last:
                        e.wait_ge(sem["rw2"], ja + 2)
                        e.activation(rf(R_V), rf(R_S1), AF.Sqrt,
                                     bias=epsc[0:1, 0:1]).then_inc(sem["sq2"], 1)
                    if 30 <= f <= 31 and not last:
                        affine(j + 1, f - 30)
            if not last:
                for c in range((4 if ph == 0 else 2), NCH):
                    affine(j + 1, c)
        j = ntiles - 1
        ja = base + j
        for c in range(NCH):
            xmove(ja, c)
        e.wait_ge(sem["x1a"], 8 * (ja + 1))
        if ph == 0:
            e.dma_start(out=x1sp[j], in_=xsl(j)).then_inc(sp("x1sp", j), 16)
            if _STAGE == "A":
                e.dma_start(out=outh[j], in_=xsl(j)).then_inc(sp("outw", j), 16)
            for c in range(NCH):
                if c <= 1:
                    e.wait_ge(sem["s2mm"], 8 * (ja + 1))
                else:
                    e.wait_ge(sem["s2mmB"], 8 * j + c - 1)
                e.activation(sqsl(c), xch(j, c), AF.Square).then_inc(
                    sem["sqB"], 1)
            e.wait_ge(sem["rwB1"], j + 1)
            e.activation(rf(R_S1), rf(R_V), AF.Sqrt, bias=epsc[0:1, 0:1]).then_inc(
                sem["sqB1"], 1)
            e.wait_ge(sem["rwB2"], j + 1)
            e.activation(rf(R_S2), rf(R_S1), AF.Sqrt, bias=epsc[0:1, 0:1]).then_inc(
                sem["sqB2"], 1)
            e.wait_ge(sem["rowB"], j + 1)
            e.dma_start(out=stsp[j:j + 1, 0:1024],
                        in_=rowsB[0:1, 0:1024]).then_inc(sp("stw", j), 16)
        else:
            e.dma_start(out=outh[j][:, 0:4 * T],
                        in_=xsl(j)[:, 0:4 * T]).then_inc(sp("owA", j), 16)
            e.dma_start(out=outh[j][:, 4 * T:8 * T],
                        in_=xsl(j)[:, 4 * T:8 * T]).then_inc(sp("owB", j), 16)

    def emit_scalar(e):
        e.wait_ge(sem["cst1"], 1)
        e.wait_ge(sem["cst0"], 96)
        if not ST_A:
            return
        emit_ffn_phase_act(e, 0, NTA)
        if not ST_B:
            return
        def affB(jt, c):
            e.wait_ge(sem["h2t"], 16 * jt + 2 * (c + 1))
            e.activation(h1ch(jt, c), tscr[:, (c % 2) * T:(c % 2 + 1) * T],
                         AF.Identity, bias=bac[:, c:c + 1],
                         scale=gac[:, c:c + 1]).then_inc(sem["h2a"], 1)

        for c in range(NCH):
            affB(0, c)
        for j in range(NT):
            for p in range(NE):
                i = j * NE + p
                e.wait_ge(sem["qkvm"], i + 1)
                if p == 0 and j >= 1:
                    e.wait_ge(sem["kna"], j)
                if p == 8 and j >= 1:
                    e.wait_ge(sem["ropQ"], j)
                if p == 12:
                    e.wait_ge(sem["qnsp_"], QSPF * j)
                    if j >= 1:
                        e.wait_ge(sem["qnv"], j)
                if p == 16 and j >= 1:
                    e.wait_ge(sem["knvd"], j)
                if p < 8:
                    dst = a2K(p)
                elif p < 16:
                    dst = a2Q(p - 8)
                else:
                    dst = a2V(p - 16)
                e.activation(dst, rot_slot(i), AF.Copy).then_inc(sem["qkve"], 1)
                if 1 <= p <= 8 and j < NT - 1:
                    affB(j + 1, p - 1)
                if 2 <= p <= 9:
                    c = p - 2
                    if j == 0 and c <= 1:
                        e.wait_ge(sem["s2mmB"], 8 * NT)
                    if c >= 2:
                        e.wait_ge(sem["s2K"], 8 * j + c - 1)
                    e.activation(sqsl(c), a2K(c), AF.Square).then_inc(
                        sem["sqK"], 1)
                if 11 <= p <= 18:
                    c = p - 11
                    if c <= 1:
                        e.wait_ge(sem["s2K"], 8 * j + 7 + c)
                    else:
                        e.wait_ge(sem["s2Q"], 8 * j + c - 1)
                    e.activation(sqsl(c), a2Q(c), AF.Square).then_inc(
                        sem["sqQ"], 1)
                if p == 19:
                    e.wait_ge(sem["s2K"], 8 * (j + 1))
                    e.wait_ge(sem["s2Q"], 8 * (j + 1))
                    e.activation(rf(R_M), stp[0:1, 0:T], AF.Sqrt,
                                 bias=l2c[0:1, 0:1])
                    e.activation(rf(R_V), stp[0:1, T:2 * T], AF.Sqrt,
                                 bias=l2c[0:1, 0:1]).then_inc(sem["sqKQ"], 1)
        if not ST_D1:
            return
        for j in range(NT):
            for ec in range(NCH):
                i = j * NCH + ec
                e.wait_ge(sem["ym"], i + 1)
                if ec == 0 and j == 0:
                    e.wait_ge(sem["qnsp_"], QSPF * NT)
                if ec == 0 and j >= 2:
                    e.wait_ge(sp("ysp", j), pcnt(j - 2))
                e.activation(a2Y(j, ec), rot10(i), AF.Copy).then_inc(
                    sem["ye"], 1)
            e.dma_start(out=ysp[j], in_=a2Ysl(j)).then_inc(sp("ysp", j), 16)
            if _STAGE == "D1":
                e.dma_start(out=outh[j], in_=a2Ysl(j)).then_inc(sp("outw", j), 16)
        if not ST_D2:
            return
        emit_ffn_phase_act(e, 1, NT)

    with nc.Block() as block:
        @block.sync
        def _(eng):
            emit_sync(eng)

        @block.tensor
        def _(eng):
            emit_tensor(eng)

        @block.vector
        def _(eng):
            emit_vector(eng)

        @block.scalar
        def _(eng):
            emit_scalar(eng)

        @block.gpsimd
        def _(eng):
            emit_gpsimd(eng)

    return nc


def _perm():
    p = []
    for blk in range(2):
        base = blk * 256
        p += list(range(base, base + 256, 2))
        p += list(range(base + 1, base + 256, 2))
    p += list(range(512, 1024))
    return np.array(p, dtype=np.int64)


def _prep(inputs):
    X = np.ascontiguousarray(np.asarray(inputs["L0_attn_X"], dtype=np.float32))
    W_in = np.asarray(inputs["L0_attn_W_in"], dtype=np.float32)
    W_out = np.asarray(inputs["L0_attn_W_out"], dtype=np.float32)
    perm = _perm()
    pfull = np.concatenate([perm, 1024 + perm, 2048 + perm])
    win_prep = np.ascontiguousarray(W_in[pfull, :].T.astype(np.float32))
    wout_prep = np.ascontiguousarray(W_out[:, perm].T.astype(np.float32))

    def cols(v):
        return np.ascontiguousarray(np.asarray(v, dtype=np.float32).reshape(8, 128).T)

    b1cols = np.ascontiguousarray(
        np.asarray(inputs["L0_ffn_fc1_bias"], dtype=np.float32).reshape(32, 128).T)
    b2cols = cols(inputs["L0_ffn_fc2_bias"])
    gf = cols(inputs["L0_ffn_ln_gamma"])
    bf = cols(inputs["L0_ffn_ln_beta"])
    ga = cols(inputs["L0_attn_ln_g"])
    ba = cols(inputs["L0_attn_ln_b"])
    q1 = np.ascontiguousarray(np.asarray(inputs["L0_ffn_fc1_q"], dtype=np.int32))
    s1 = np.ascontiguousarray(np.asarray(inputs["L0_ffn_fc1_scales"], dtype=np.float32))
    q2 = np.ascontiguousarray(np.asarray(inputs["L0_ffn_fc2_q"], dtype=np.int32))
    s2 = np.ascontiguousarray(np.asarray(inputs["L0_ffn_fc2_scales"], dtype=np.float32))

    inv = (np.float32(1.0) /
           (np.float32(10000.0) **
            (np.arange(0, 512, 2, dtype=np.float32) / np.float32(512.0)))).astype(np.float32)

    in_maps = []
    for c in range(8):
        xs = X[:, c * POS:(c + 1) * POS, :]
        xs = xs.reshape(2, 8, T, NCH, 128)
        xs = np.ascontiguousarray(xs.transpose(0, 1, 4, 3, 2))
        xs = np.ascontiguousarray(xs.reshape(NT, 128, NCH * T))
        pos = np.arange(c * POS, (c + 1) * POS, dtype=np.float32).reshape(8, T)
        cst = np.empty((8, 128, 4 * T), dtype=np.float32)
        for jt in range(8):
            ang = (pos[jt][None, :] * inv[:, None]).astype(np.float32)
            cosv = np.cos(ang).astype(np.float32)
            sinv = np.sin(ang).astype(np.float32)
            cst[jt, :, 0:T] = cosv[0:128]
            cst[jt, :, T:2 * T] = cosv[128:256]
            cst[jt, :, 2 * T:3 * T] = sinv[0:128]
            cst[jt, :, 3 * T:4 * T] = sinv[128:256]
        in_maps.append({
            "xin": xs, "q1": q1, "s1": s1, "b1": b1cols,
            "q2": q2, "s2": s2, "b2": b2cols,
            "gf": gf, "bf": bf, "ga": ga, "ba": ba,
            "win": win_prep, "wout": wout_prep, "cst": cst,
        })
    return in_maps


def kernel(**inputs):
    in_maps = _prep(inputs)
    nc = _build()
    res = run_bass_kernel_spmd(nc, in_maps, core_ids=list(range(8)))
    out = np.empty((2, 16384, 1024), dtype=np.float32)
    for c in range(8):
        o = np.asarray(res.results[c]["outp"])
        o = o.reshape(2, 8, 128, NCH, T).transpose(0, 1, 4, 3, 2)
        out[:, c * POS:(c + 1) * POS, :] = o.reshape(2, POS, D)
    return out
